# version 03
# baseline (speedup 1.0000x reference)
from contextlib import ExitStack

import numpy as np
import concourse.bass as bass
import concourse.mybir as mybir
from concourse.bass_utils import run_bass_kernel_spmd

F32 = mybir.dt.float32
I32 = mybir.dt.int32
AF = mybir.ActivationFunctionType
ALU = mybir.AluOpType
AX = mybir.AxisListType

B, T, D, H, DK, DV = 2, 4096, 2048, 16, 128, 128
BT = B * T
PC = BT // 8
HLO = 16
CH = 128
NCH = T // CH
M_CL = 85.0
HID = 5632
NHC = HID // 128


class Seq:
    def __init__(self, nc, st):
        self.sem = st.enter_context(nc.semaphore("seq"))
        self.cnt = 0

    def s(self, eng, make, dma=False):
        if self.cnt:
            eng.wait_ge(self.sem, self.cnt)
        make().then_inc(self.sem, 16 if dma else 1)
        self.cnt += 16 if dma else 1

    def fin(self, eng):
        eng.wait_ge(self.sem, self.cnt)


def build_L1():
    nc = bass.Bass("TRN2", target_bir_lowering=False)
    TW = PC + HLO
    xT = nc.dram_tensor("xT", [D, TW], F32, kind="ExternalInput").ap()
    wq = nc.dram_tensor("wq", [D, D], F32, kind="ExternalInput").ap()
    wk = nc.dram_tensor("wk", [D, D], F32, kind="ExternalInput").ap()
    wv = nc.dram_tensor("wv", [D, D], F32, kind="ExternalInput").ap()
    cw = nc.dram_tensor("cw", [3 * D, 4], F32, kind="ExternalInput").ap()
    unc = nc.dram_tensor("unc", [D, 64], F32, kind="ExternalInput").ap()
    g1 = nc.dram_tensor("g1", [D, DV], F32, kind="ExternalInput").ap()
    prox = nc.dram_tensor("prox", [DK, DV], F32, kind="ExternalInput").ap()
    outs = {}
    for nm, rows in (("qs_o", D), ("k_o", D), ("v_o", D), ("vh_o", D),
                     ("lg_o", 64), ("xg_o", DV)):
        outs[nm] = nc.dram_tensor(nm, [rows, PC], F32, kind="ExternalOutput").ap()
    with ExitStack() as st:
        sq = Seq(nc, st)
        xs = st.enter_context(nc.sbuf_tensor("xs", [128, 16 * TW], F32))
        ws = st.enter_context(nc.sbuf_tensor("ws", [128, 2048], F32))
        cwsb = st.enter_context(nc.sbuf_tensor("cwsb", [128, 4], F32))
        cbuf = st.enter_context(nc.sbuf_tensor("cbuf", [128, TW], F32))
        obuf = st.enter_context(nc.sbuf_tensor("obuf", [128, TW], F32))
        ps = st.enter_context(nc.psum_tensor("ps", [128, 512], F32))
        ps2 = st.enter_context(nc.psum_tensor("ps2", [128, 512], F32))
        block = st.enter_context(nc.Block())
        NT = [(0, 512), (512, 512), (1024, TW - 1024)]

        @block.sync
        def _(sy):
            for kc in range(16):
                sq.s(sy, lambda kc=kc: sy.dma_start(
                    out=xs[:, kc * TW:(kc + 1) * TW],
                    in_=xT[kc * 128:(kc + 1) * 128, :]), dma=True)

        def proj(w_ap, conv_row, out_ap, scale, width=128, oc_n=16, tok0=HLO):
            for oc in range(oc_n):
                @block.sync
                def _(sy, oc=oc):
                    for kc in range(16):
                        sq.s(sy, lambda kc=kc: sy.dma_start(
                            out=ws[:, kc * width:kc * width + width],
                            in_=w_ap[kc * 128:(kc + 1) * 128,
                                     oc * width:oc * width + width]), dma=True)
                    if conv_row is not None:
                        sq.s(sy, lambda: sy.dma_start(
                            out=cwsb[:, :],
                            in_=cw[conv_row + oc * 128:conv_row + (oc + 1) * 128, :]),
                            dma=True)

                for nt, (o0, nn) in enumerate(NT):
                    @block.tensor
                    def _(t, oc=oc, o0=o0, nn=nn, nt=nt):
                        for kc in range(16):
                            sq.s(t, lambda kc=kc: t.matmul(
                                (ps if nt % 2 == 0 else ps2)[0:width, 0:nn],
                                ws[:, kc * width:kc * width + width],
                                xs[:, kc * TW + o0:kc * TW + o0 + nn],
                                start=(kc == 0), stop=(kc == 15)))

                    @block.vector
                    def _(v, oc=oc, o0=o0, nn=nn, nt=nt):
                        sq.s(v, lambda: v.tensor_copy(
                            cbuf[0:width, o0:o0 + nn],
                            (ps if nt % 2 == 0 else ps2)[0:width, 0:nn]))

                if conv_row is None:
                    @block.gpsimd
                    def _(g, oc=oc):
                        sq.s(g, lambda: g.dma_start(
                            out=out_ap[oc * width:oc * width + width, :],
                            in_=cbuf[0:width, tok0:tok0 + PC]), dma=True)
                else:
                    @block.scalar
                    def _(s, oc=oc):
                        sq.s(s, lambda: s.activation(
                            obuf[:, 3:TW], cbuf[:, 3:TW], AF.Copy,
                            scale=cwsb[:, 3:4]))

                    @block.vector
                    def _(v, oc=oc):
                        for j in (2, 1, 0):
                            sq.s(v, lambda j=j: v.scalar_tensor_tensor(
                                obuf[:, 3:TW], cbuf[:, j:TW - 3 + j],
                                cwsb[:, j:j + 1], obuf[:, 3:TW],
                                ALU.mult, ALU.add))

                    @block.scalar
                    def _(s, oc=oc):
                        sq.s(s, lambda: s.activation(
                            obuf[:, HLO:TW], obuf[:, HLO:TW], AF.Silu))
                    if scale != 1.0:
                        @block.vector
                        def _(v, oc=oc):
                            sq.s(v, lambda: v.tensor_scalar_mul(
                                obuf[:, HLO:TW], obuf[:, HLO:TW], scale))

                    @block.gpsimd
                    def _(g, oc=oc):
                        sq.s(g, lambda: g.dma_start(
                            out=out_ap[oc * 128:(oc + 1) * 128, :],
                            in_=obuf[:, HLO:TW]), dma=True)

        proj(wq, 0, outs["qs_o"], DK ** -0.5)
        proj(wk, D, outs["k_o"], 1.0)
        proj(wv, 2 * D, outs["v_o"], 1.0)
        proj(unc, None, outs["lg_o"], 1.0, width=64, oc_n=1)
        proj(g1, None, outs["xg_o"], 1.0, width=128, oc_n=1)

        @block.sync
        def _(sy):
            sq.s(sy, lambda: sy.dma_start(out=ws[:, 0:DV], in_=prox), dma=True)
        for h in range(H):
            @block.gpsimd
            def _(g, h=h):
                sq.s(g, lambda: g.dma_start(
                    out=cbuf[:, 0:PC],
                    in_=outs["k_o"][h * DK:(h + 1) * DK, :]), dma=True)

            @block.tensor
            def _(t, h=h):
                for nt in range(2):
                    sq.s(t, lambda nt=nt: t.matmul(
                        (ps if nt == 0 else ps2)[:, 0:512],
                        ws[:, 0:DV], cbuf[:, nt * 512:nt * 512 + 512],
                        start=True, stop=True))

            @block.vector
            def _(v, h=h):
                for nt in range(2):
                    sq.s(v, lambda nt=nt: v.tensor_copy(
                        obuf[:, nt * 512:nt * 512 + 512],
                        (ps if nt == 0 else ps2)[:, 0:512]))

            @block.gpsimd
            def _(g, h=h):
                sq.s(g, lambda: g.dma_start(
                    out=outs["vh_o"][h * DV:(h + 1) * DV, :],
                    in_=obuf[:, 0:PC]), dma=True)

        @block.sync
        def _(sy):
            sq.fin(sy)
    return nc


def build_L3():
    nc = bass.Bass("TRN2", target_bir_lowering=False)
    k_t = nc.dram_tensor("k_t", [PC, D], F32, kind="ExternalInput").ap()
    v_t = nc.dram_tensor("v_t", [PC, D], F32, kind="ExternalInput").ap()
    vh_t = nc.dram_tensor("vh_t", [PC, D], F32, kind="ExternalInput").ap()
    lg_t = nc.dram_tensor("lg_t", [PC, 64], F32, kind="ExternalInput").ap()
    ku_o = nc.dram_tensor("ku_o", [PC, D], F32, kind="ExternalOutput").ap()
    f4_o = nc.dram_tensor("f4_o", [PC, 48], F32, kind="ExternalOutput").ap()
    ent_o = nc.dram_tensor("ent_o", [PC, 1], F32, kind="ExternalOutput").ap()
    with ExitStack() as st:
        sq = Seq(nc, st)
        kt = st.enter_context(nc.sbuf_tensor("kt", [128, D], F32))
        vt = st.enter_context(nc.sbuf_tensor("vt", [128, D], F32))
        vht = st.enter_context(nc.sbuf_tensor("vht", [128, D], F32))
        er = st.enter_context(nc.sbuf_tensor("er", [128, D], F32))
        sr = st.enter_context(nc.sbuf_tensor("sr", [128, D], F32))
        lg = st.enter_context(nc.sbuf_tensor("lg", [128, 64], F32))
        ex = st.enter_context(nc.sbuf_tensor("ex", [128, 64], F32))
        co = st.enter_context(nc.sbuf_tensor("co", [128, 128], F32))
        ft = st.enter_context(nc.sbuf_tensor("ft", [128, 48], F32))
        block = st.enter_context(nc.Block())

        for tile in range(PC // 128):
            r = slice(tile * 128, (tile + 1) * 128)

            @block.sync
            def _(sy, r=r):
                sq.s(sy, lambda: sy.dma_start(out=kt[:], in_=k_t[r, :]), dma=True)
                sq.s(sy, lambda: sy.dma_start(out=vt[:], in_=v_t[r, :]), dma=True)
                sq.s(sy, lambda: sy.dma_start(out=vht[:], in_=vh_t[r, :]), dma=True)
                sq.s(sy, lambda: sy.dma_start(out=lg[:], in_=lg_t[r, :]), dma=True)

            @block.vector
            def _(v):
                sq.s(v, lambda: v.tensor_sub(er[:], vt[:], vht[:]))
            for src, c0 in ((kt, 0), (vt, 16), (vht, 32), (er, 48)):
                @block.scalar
                def _(s, src=src):
                    sq.s(s, lambda: s.activation(sr[:], src[:], AF.Square))

                @block.vector
                def _(v, c0=c0):
                    for h in range(H):
                        sq.s(v, lambda h=h: v.reduce_sum(
                            co[:, c0 + h:c0 + h + 1],
                            sr[:, h * DK:(h + 1) * DK], AX.X))
            @block.vector
            def _(v):
                sq.s(v, lambda: v.reduce_max(co[:, 66:67], lg[:], AX.X))
                sq.s(v, lambda: v.tensor_scalar(
                    ex[:], lg[:], co[:, 66:67], None, ALU.subtract))

            @block.scalar
            def _(s):
                sq.s(s, lambda: s.activation(
                    ex[:], ex[:], AF.Exp, accum_out=co[:, 64:65]))

            @block.vector
            def _(v):
                sq.s(v, lambda: v.tensor_mul(ex[:], ex[:], lg[:]))
                sq.s(v, lambda: v.reduce_sum(co[:, 65:66], ex[:], AX.X))
                sq.s(v, lambda: v.reciprocal(co[:, 68:69], co[:, 64:65]))

            @block.scalar
            def _(s):
                sq.s(s, lambda: s.activation(co[:, 69:70], co[:, 64:65], AF.Ln))

            @block.vector
            def _(v):
                sq.s(v, lambda: v.tensor_mul(co[:, 70:71], co[:, 65:66], co[:, 68:69]))
                sq.s(v, lambda: v.tensor_add(co[:, 69:70], co[:, 69:70], co[:, 66:67]))
                sq.s(v, lambda: v.tensor_sub(co[:, 67:68], co[:, 69:70], co[:, 70:71]))
                sq.s(v, lambda: v.tensor_scalar_mul(
                    co[:, 67:68], co[:, 67:68], float(1.0 / np.log(64.0))))

            @block.scalar
            def _(s):
                sq.s(s, lambda: s.activation(co[:, 0:64], co[:, 0:64], AF.Ln))
                sq.s(s, lambda: s.activation(co[:, 0:64], co[:, 0:64], AF.Exp, scale=0.5))
            @block.vector
            def _(v):
                sq.s(v, lambda: v.tensor_scalar_add(co[:, 96:112], co[:, 0:16], 1e-6))
                sq.s(v, lambda: v.reciprocal(co[:, 96:112], co[:, 96:112]))
                for h in range(H):
                    sq.s(v, lambda h=h: v.tensor_scalar(
                        kt[:, h * DK:(h + 1) * DK], kt[:, h * DK:(h + 1) * DK],
                        co[:, 96 + h:97 + h], None, ALU.mult))

            @block.gpsimd
            def _(g, r=r):
                sq.s(g, lambda: g.dma_start(out=ku_o[r, :], in_=kt[:]), dma=True)

            @block.vector
            def _(v):
                sq.s(v, lambda: v.tensor_scalar_add(co[:, 112:128], co[:, 16:32], 1e-6))
                sq.s(v, lambda: v.reciprocal(co[:, 112:128], co[:, 112:128]))
                sq.s(v, lambda: v.tensor_mul(ft[:, 0:16], co[:, 48:64], co[:, 112:128]))
                sq.s(v, lambda: v.tensor_mul(co[:, 68:84], co[:, 16:32], co[:, 16:32]))
                sq.s(v, lambda: v.tensor_mul(co[:, 84:100], co[:, 32:48], co[:, 32:48]))
                sq.s(v, lambda: v.tensor_add(co[:, 68:84], co[:, 68:84], co[:, 84:100]))
                sq.s(v, lambda: v.tensor_mul(co[:, 84:100], co[:, 48:64], co[:, 48:64]))
                sq.s(v, lambda: v.tensor_sub(co[:, 68:84], co[:, 68:84], co[:, 84:100]))
                sq.s(v, lambda: v.tensor_scalar_mul(co[:, 68:84], co[:, 68:84], 0.5))
                sq.s(v, lambda: v.tensor_mul(co[:, 84:100], co[:, 16:32], co[:, 32:48]))
                sq.s(v, lambda: v.tensor_scalar_add(co[:, 84:100], co[:, 84:100], 1e-6))
                sq.s(v, lambda: v.reciprocal(co[:, 84:100], co[:, 84:100]))
                sq.s(v, lambda: v.tensor_mul(ft[:, 16:32], co[:, 68:84], co[:, 84:100]))

            @block.scalar
            def _(s):
                sq.s(s, lambda: s.activation(ft[:, 32:48], co[:, 48:64], AF.Ln, bias=1.0))

            @block.gpsimd
            def _(g, r=r):
                sq.s(g, lambda: g.dma_start(out=f4_o[r, :], in_=ft[:]), dma=True)
                sq.s(g, lambda: g.dma_start(out=ent_o[r, :], in_=co[:, 67:68]), dma=True)

        @block.sync
        def _(sy):
            sq.fin(sy)
    return nc


def build_L4():
    FW = H * BT // 8
    nc = bass.Bass("TRN2", target_bir_lowering=False)
    f8 = nc.dram_tensor("f8", [8, FW], F32, kind="ExternalInput").ap()
    w1 = nc.dram_tensor("w1", [8, 32], F32, kind="ExternalInput").ap()
    b1 = nc.dram_tensor("b1", [32, 1], F32, kind="ExternalInput").ap()
    w2 = nc.dram_tensor("w2", [32, 32], F32, kind="ExternalInput").ap()
    b2 = nc.dram_tensor("b2", [32, 1], F32, kind="ExternalInput").ap()
    wst = nc.dram_tensor("wst", [32, 5], F32, kind="ExternalInput").ap()
    b5 = nc.dram_tensor("b5", [5, 1], F32, kind="ExternalInput").ap()
    cmb = nc.dram_tensor("cmb", [5, 5], F32, kind="ExternalInput").ap()
    sg_o = nc.dram_tensor("sg_o", [5, FW], F32, kind="ExternalOutput").ap()
    with ExitStack() as st:
        sq = Seq(nc, st)
        fs = st.enter_context(nc.sbuf_tensor("fs", [8, 4096], F32))
        h1 = st.enter_context(nc.sbuf_tensor("h1", [32, 4096], F32))
        hf = st.enter_context(nc.sbuf_tensor("hf", [32, 4096], F32))
        r5 = st.enter_context(nc.sbuf_tensor("r5", [5, 4096], F32))
        sg = st.enter_context(nc.sbuf_tensor("sg", [5, 4096], F32))
        wsb = st.enter_context(nc.sbuf_tensor("wsb", [32, 80], F32))
        ps = st.enter_context(nc.psum_tensor("ps", [128, 512], F32))
        block = st.enter_context(nc.Block())

        @block.sync
        def _(sy):
            sq.s(sy, lambda: sy.dma_start(out=wsb[0:8, 0:32], in_=w1), dma=True)
            sq.s(sy, lambda: sy.dma_start(out=wsb[0:32, 32:64], in_=w2), dma=True)
            sq.s(sy, lambda: sy.dma_start(out=wsb[0:32, 64:69], in_=wst), dma=True)
            sq.s(sy, lambda: sy.dma_start(out=wsb[0:5, 69:74], in_=cmb), dma=True)
            sq.s(sy, lambda: sy.dma_start(out=wsb[0:32, 74:75], in_=b1), dma=True)
            sq.s(sy, lambda: sy.dma_start(out=wsb[0:32, 75:76], in_=b2), dma=True)
            sq.s(sy, lambda: sy.dma_start(out=wsb[0:5, 76:77], in_=b5), dma=True)

        for til in range(FW // 4096):
            c0 = til * 4096

            @block.sync
            def _(sy, c0=c0):
                sq.s(sy, lambda: sy.dma_start(
                    out=fs[:], in_=f8[:, c0:c0 + 4096]), dma=True)
            for sub in range(8):
                o0 = sub * 512

                @block.tensor
                def _(t, o0=o0):
                    sq.s(t, lambda: t.matmul(
                        ps[0:32, 0:512], wsb[0:8, 0:32], fs[:, o0:o0 + 512],
                        start=True, stop=True))

                @block.scalar
                def _(s, o0=o0):
                    sq.s(s, lambda: s.activation(
                        h1[:, o0:o0 + 512], ps[0:32, 0:512], AF.Silu,
                        bias=wsb[0:32, 74:75]))

                @block.tensor
                def _(t, o0=o0):
                    sq.s(t, lambda: t.matmul(
                        ps[0:32, 0:512], wsb[0:32, 32:64], h1[:, o0:o0 + 512],
                        start=True, stop=True))

                @block.scalar
                def _(s, o0=o0):
                    sq.s(s, lambda: s.activation(
                        hf[:, o0:o0 + 512], ps[0:32, 0:512], AF.Silu,
                        bias=wsb[0:32, 75:76]))

                @block.tensor
                def _(t, o0=o0):
                    sq.s(t, lambda: t.matmul(
                        ps[0:5, 0:512], wsb[0:32, 64:69], hf[:, o0:o0 + 512],
                        start=True, stop=True))

                @block.vector
                def _(v, o0=o0):
                    sq.s(v, lambda: v.tensor_scalar(
                        r5[:, o0:o0 + 512], ps[0:5, 0:512], wsb[0:5, 76:77],
                        None, ALU.add))

                @block.tensor
                def _(t, o0=o0):
                    sq.s(t, lambda: t.matmul(
                        ps[0:5, 0:512], wsb[0:5, 69:74], r5[:, o0:o0 + 512],
                        start=True, stop=True))

                @block.scalar
                def _(s, o0=o0):
                    sq.s(s, lambda: s.activation(
                        sg[:, o0:o0 + 512], ps[0:5, 0:512], AF.Tanh, scale=0.5))

                @block.vector
                def _(v, o0=o0):
                    sq.s(v, lambda: v.tensor_scalar(
                        sg[:, o0:o0 + 512], sg[:, o0:o0 + 512], 0.5, 0.5,
                        ALU.mult, ALU.add))

            @block.gpsimd
            def _(g, c0=c0):
                sq.s(g, lambda: g.dma_start(
                    out=sg_o[:, c0:c0 + 4096], in_=sg[:]), dma=True)

        @block.sync
        def _(sy):
            sq.fin(sy)
    return nc


def build_L5():
    nc = bass.Bass("TRN2", target_bir_lowering=False)
    kin = nc.dram_tensor("kin", [512, T], F32, kind="ExternalInput").ap()
    qin = nc.dram_tensor("qin", [512, T], F32, kind="ExternalInput").ap()
    vin = nc.dram_tensor("vin", [512, T], F32, kind="ExternalInput").ap()
    ampc = nc.dram_tensor("ampc", [128, 8 * NCH], F32, kind="ExternalInput").ap()
    betc = nc.dram_tensor("betc", [128, 8 * NCH], F32, kind="ExternalInput").ap()
    alogc = nc.dram_tensor("alogc", [128, 2], F32, kind="ExternalInput").ap()
    dtbc = nc.dram_tensor("dtbc", [128, 2], F32, kind="ExternalInput").ap()
    am = nc.dram_tensor("am", [10 * 128, 128], F32, kind="ExternalInput").ap()
    cm = nc.dram_tensor("cm", [10 * 128, 128], F32, kind="ExternalInput").ap()
    onest = nc.dram_tensor("onest", [128, 128], F32, kind="ExternalInput").ap()
    ident = nc.dram_tensor("ident", [128, 128], F32, kind="ExternalInput").ap()
    o_o = nc.dram_tensor("o_o", [8 * T, 128], F32, kind="ExternalOutput").ap()
    with ExitStack() as st:
        sq = Seq(nc, st)
        sb = lambda nm, shp: st.enter_context(nc.sbuf_tensor(nm, shp, F32))
        kc_ = sb("kc_", [128, 128]); qc_ = sb("qc_", [128, 128]); vc_ = sb("vc_", [128, 128])
        ampsb = sb("ampsb", [128, 8 * NCH]); betsb = sb("betsb", [128, 8 * NCH])
        gcol = sb("gcol", [128, 2]); gpos = sb("gpos", [128, 2])
        t1 = sb("t1", [128, 2]); t2 = sb("t2", [128, 2])
        amsb = sb("amsb", [128, 10 * 128]); cmsb = sb("cmsb", [128, 10 * 128])
        onesb = sb("onesb", [128, 128]); idsb = sb("idsb", [128, 128])
        ampb = sb("ampb", [128, 128])
        Dt = sb("Dt", [128, 11 * 128])
        Mt = sb("Mt", [128, 16 * 128])
        Nt = sb("Nt", [128, 128]); Aqt = sb("Aqt", [128, 128]); tmp = sb("tmp", [128, 128])
        pw = sb("pw", [128, 13 * 128])
        zz = sb("zz", [128, 128]); osb = sb("osb", [128, 128])
        khstb = sb("khstb", [128, 128]); s0d = sb("s0d", [128, 128])
        Ssb = sb("Ssb", [128, 8 * 128])
        psa = st.enter_context(nc.psum_tensor("psa", [128, 128], F32))
        psb = st.enter_context(nc.psum_tensor("psb", [128, 128], F32))
        psc = st.enter_context(nc.psum_tensor("psc", [128, 128], F32))
        block = st.enter_context(nc.Block())

        @block.sync
        def _(sy):
            sq.s(sy, lambda: sy.dma_start(out=ampsb[:], in_=ampc), dma=True)
            sq.s(sy, lambda: sy.dma_start(out=betsb[:], in_=betc), dma=True)
            sq.s(sy, lambda: sy.dma_start(out=t1[:], in_=alogc), dma=True)
            sq.s(sy, lambda: sy.dma_start(out=t2[:], in_=dtbc), dma=True)
            sq.s(sy, lambda: sy.dma_start(out=amsb[:], in_=am.rearrange("(a p) f -> p a f", p=128)), dma=True)
            sq.s(sy, lambda: sy.dma_start(out=cmsb[:], in_=cm.rearrange("(a p) f -> p a f", p=128)), dma=True)
            sq.s(sy, lambda: sy.dma_start(out=onesb[:], in_=onest), dma=True)
            sq.s(sy, lambda: sy.dma_start(out=idsb[:], in_=ident), dma=True)

        @block.scalar
        def _(s):
            sq.s(s, lambda: s.activation(gcol[:], t2[:], AF.Abs))
            sq.s(s, lambda: s.activation(gcol[:], gcol[:], AF.Exp, scale=-1.0))
            sq.s(s, lambda: s.activation(gcol[:], gcol[:], AF.Ln, bias=1.0))

        @block.vector
        def _(v):
            sq.s(v, lambda: v.tensor_relu(gpos[:], t2[:]))
            sq.s(v, lambda: v.tensor_add(gpos[:], gpos[:], gcol[:]))

        @block.scalar
        def _(s):
            sq.s(s, lambda: s.activation(t1[:], t1[:], AF.Exp))

        @block.vector
        def _(v):
            sq.s(v, lambda: v.tensor_mul(gpos[:], gpos[:], t1[:]))
            sq.s(v, lambda: v.tensor_scalar_mul(gcol[:], gpos[:], -1.0))
            sq.s(v, lambda: v.memset(Ssb[:], 0.0))

        for c in range(NCH):
            for s_ in range(4):
                hl = s_ // 2

                @block.sync
                def _(sy, c=c, s_=s_):
                    sq.s(sy, lambda: sy.dma_start(
                        out=kc_[:], in_=kin[s_ * 128:(s_ + 1) * 128, c * CH:(c + 1) * CH]), dma=True)
                    sq.s(sy, lambda: sy.dma_start(
                        out=qc_[:], in_=qin[s_ * 128:(s_ + 1) * 128, c * CH:(c + 1) * CH]), dma=True)
                    sq.s(sy, lambda: sy.dma_start(
                        out=vc_[:], in_=vin[s_ * 128:(s_ + 1) * 128, c * CH:(c + 1) * CH]), dma=True)
                for fs in range(2):
                    inst = s_ * 2 + fs
                    acol = ampsb[:, inst * NCH + c:inst * NCH + c + 1]
                    bcol = betsb[:, inst * NCH + c:inst * NCH + c + 1]
                    gc = gcol[:, hl:hl + 1]
                    gp = gpos[:, hl:hl + 1]
                    S = Ssb[:, inst * 128:(inst + 1) * 128]

                    @block.scalar
                    def _(s, acol=acol):
                        sq.s(s, lambda: s.activation(ampb[:], onesb[:], AF.Copy, scale=acol))
                    for ai in range(10):
                        @block.tensor
                        def _(t, ai=ai):
                            sq.s(t, lambda: t.matmul(
                                (psa if ai % 2 == 0 else psb)[:],
                                ampb[:], amsb[:, ai * 128:(ai + 1) * 128],
                                start=True, stop=True))

                        @block.scalar
                        def _(s, ai=ai, gc=gc):
                            sq.s(s, lambda: s.activation(
                                Dt[:, ai * 128:(ai + 1) * 128],
                                (psa if ai % 2 == 0 else psb)[:], AF.Exp, scale=gc))
                        if ai == 3:
                            @block.vector
                            def _(v, gp=gp):
                                sq.s(v, lambda: v.tensor_scalar(
                                    tmp[:], psb[:], gp, M_CL, ALU.mult, ALU.min))

                            @block.scalar
                            def _(s):
                                sq.s(s, lambda: s.activation(
                                    Dt[:, 10 * 128:11 * 128], tmp[:], AF.Exp))
                    @block.vector
                    def _(v):
                        for l in range(4):
                            sq.s(v, lambda l=l: v.scalar_tensor_tensor(
                                Mt[:, l * 128:(l + 1) * 128], kc_[:], -1.0,
                                Dt[:, l * 128:(l + 1) * 128], ALU.mult, ALU.mult))
                            sq.s(v, lambda l=l: v.tensor_mul(
                                Mt[:, (4 + l) * 128:(5 + l) * 128], qc_[:],
                                Dt[:, l * 128:(l + 1) * 128]))
                            sq.s(v, lambda l=l: v.tensor_mul(
                                Mt[:, (8 + l) * 128:(9 + l) * 128], kc_[:],
                                Dt[:, (4 + l) * 128:(5 + l) * 128]))
                        sq.s(v, lambda: v.tensor_mul(
                            Mt[:, 12 * 128:13 * 128], kc_[:], Dt[:, 10 * 128:11 * 128]))
                        sq.s(v, lambda: v.scalar_tensor_tensor(
                            Mt[:, 13 * 128:14 * 128], kc_[:], -1.0,
                            Dt[:, 8 * 128:9 * 128], ALU.mult, ALU.mult))
                        sq.s(v, lambda: v.tensor_mul(
                            Mt[:, 14 * 128:15 * 128], qc_[:], Dt[:, 8 * 128:9 * 128]))
                        sq.s(v, lambda: v.tensor_mul(
                            Mt[:, 15 * 128:16 * 128], kc_[:], Dt[:, 9 * 128:10 * 128]))
                    for which, dst in ((0, Nt), (1, Aqt)):
                        for l in range(5):
                            lhs = Mt[:, (8 + l) * 128:(9 + l) * 128] if l < 4 else Mt[:, 12 * 128:13 * 128]
                            rl = l if l < 4 else 3
                            rhs = (Mt[:, rl * 128:(rl + 1) * 128] if which == 0
                                   else Mt[:, (4 + rl) * 128:(5 + rl) * 128])
                            mi = which * 5 + l

                            @block.tensor
                            def _(t, lhs=lhs, rhs=rhs, l=l):
                                sq.s(t, lambda: t.matmul(
                                    (psa if l % 2 == 0 else psb)[:], lhs, rhs,
                                    start=True, stop=True))

                            @block.vector
                            def _(v, l=l, mi=mi, dst=dst, bcol=bcol):
                                if l == 0:
                                    sq.s(v, lambda: v.scalar_tensor_tensor(
                                        dst[:], (psa if l % 2 == 0 else psb)[:], bcol,
                                        cmsb[:, mi * 128:(mi + 1) * 128],
                                        ALU.mult, ALU.mult))
                                else:
                                    sq.s(v, lambda: v.scalar_tensor_tensor(
                                        tmp[:], (psa if l % 2 == 0 else psb)[:], bcol,
                                        cmsb[:, mi * 128:(mi + 1) * 128],
                                        ALU.mult, ALU.mult))
                                    sq.s(v, lambda: v.tensor_add(dst[:], dst[:], tmp[:]))
                    @block.tensor
                    def _(t):
                        sq.s(t, lambda: t.transpose(psc[:], Nt[:], idsb[:]))

                    @block.vector
                    def _(v):
                        sq.s(v, lambda: v.tensor_copy(pw[:, 7 * 128:8 * 128], psc[:]))
                        sq.s(v, lambda: v.tensor_copy(pw[:, 0:128], Nt[:]))
                    for kp in range(1, 7):
                        @block.tensor
                        def _(t, kp=kp):
                            sq.s(t, lambda: t.matmul(
                                psa[:], pw[:, (6 + kp) * 128:(7 + kp) * 128],
                                pw[:, (kp - 1) * 128:kp * 128], start=True, stop=True))

                        @block.vector
                        def _(v, kp=kp):
                            sq.s(v, lambda: v.tensor_copy(
                                pw[:, kp * 128:(kp + 1) * 128], psa[:]))
                        if kp < 6:
                            @block.tensor
                            def _(t, kp=kp):
                                sq.s(t, lambda: t.matmul(
                                    psb[:], pw[:, (kp - 1) * 128:kp * 128],
                                    pw[:, (6 + kp) * 128:(7 + kp) * 128],
                                    start=True, stop=True))

                            @block.vector
                            def _(v, kp=kp):
                                sq.s(v, lambda: v.tensor_copy(
                                    pw[:, (7 + kp) * 128:(8 + kp) * 128], psb[:]))
                    @block.tensor
                    def _(t):
                        sq.s(t, lambda: t.transpose(psc[:], vc_[:], idsb[:]))
                        sq.s(t, lambda: t.matmul(psc[:], Mt[:, 13 * 128:14 * 128],
                                                 S, start=False, stop=True))

                    @block.vector
                    def _(v):
                        sq.s(v, lambda: v.tensor_copy(zz[:], psc[:]))
                    for kp in (6, 5, 4, 3, 2, 1):
                        @block.tensor
                        def _(t, kp=kp):
                            sq.s(t, lambda: t.matmul(
                                psa[:], pw[:, kp * 128:(kp + 1) * 128], zz[:],
                                start=True, stop=True))

                        @block.vector
                        def _(v):
                            sq.s(v, lambda: v.tensor_add(zz[:], zz[:], psa[:]))

                    @block.tensor
                    def _(t):
                        sq.s(t, lambda: t.matmul(psa[:], pw[:, 0:128], zz[:],
                                                 start=True, stop=True))

                    @block.vector
                    def _(v):
                        sq.s(v, lambda: v.tensor_sub(zz[:], zz[:], psa[:]))
                    @block.tensor
                    def _(t):
                        sq.s(t, lambda: t.matmul(psb[:], Mt[:, 14 * 128:15 * 128], S,
                                                 start=True, stop=False))
                        sq.s(t, lambda: t.matmul(psb[:], Aqt[:], zz[:],
                                                 start=False, stop=True))

                    @block.vector
                    def _(v):
                        sq.s(v, lambda: v.tensor_copy(osb[:], psb[:]))

                    @block.gpsimd
                    def _(g, inst=inst, c=c):
                        sq.s(g, lambda: g.dma_start(
                            out=o_o[inst * T + c * CH:inst * T + (c + 1) * CH, :],
                            in_=osb[:]), dma=True)
                    @block.tensor
                    def _(t):
                        sq.s(t, lambda: t.transpose(psc[:], Mt[:, 15 * 128:16 * 128], idsb[:]))

                    @block.vector
                    def _(v, bcol=bcol):
                        sq.s(v, lambda: v.tensor_scalar(
                            khstb[:], psc[:], bcol, None, ALU.mult))
                        sq.s(v, lambda: v.tensor_scalar(
                            s0d[:], S, Dt[:, 8 * 128 + 127:8 * 128 + 128], None, ALU.mult))

                    @block.tensor
                    def _(t):
                        sq.s(t, lambda: t.matmul(psa[:], khstb[:], zz[:],
                                                 start=True, stop=False))
                        sq.s(t, lambda: t.matmul(psa[:], idsb[:], s0d[:],
                                                 start=False, stop=True))

                    @block.vector
                    def _(v, S=S):
                        sq.s(v, lambda: v.tensor_copy(S, psa[:]))

        @block.sync
        def _(sy):
            sq.fin(sy)
    return nc


def build_L6a():
    nc = bass.Bass("TRN2", target_bir_lowering=False)
    of_t = nc.dram_tensor("of_t", [PC, D], F32, kind="ExternalInput").ap()
    os_t = nc.dram_tensor("os_t", [PC, D], F32, kind="ExternalInput").ap()
    lam_t = nc.dram_tensor("lam_t", [PC, H], F32, kind="ExternalInput").ap()
    xgfm = nc.dram_tensor("xgfm", [128, PC], F32, kind="ExternalInput").ap()
    g2w = nc.dram_tensor("g2w", [128, D], F32, kind="ExternalInput").ap()
    g2bb = nc.dram_tensor("g2bb", [128, D], F32, kind="ExternalInput").ap()
    onbc = nc.dram_tensor("onbc", [128, D], F32, kind="ExternalInput").ap()
    of_o = nc.dram_tensor("of_o", [PC, D], F32, kind="ExternalOutput").ap()
    with ExitStack() as st:
        sq = Seq(nc, st)
        sb = lambda nm, shp: st.enter_context(nc.sbuf_tensor(nm, shp, F32))
        oa = sb("oa", [128, D]); ob = sb("ob", [128, D]); lam = sb("lam", [128, H])
        xg = sb("xg", [128, PC]); gw = sb("gw", [128, D]); gb = sb("gb", [128, D])
        onb = sb("onb", [128, D]); sig = sb("sig", [128, D]); co = sb("co", [128, 64])
        ps = st.enter_context(nc.psum_tensor("ps", [128, 512], F32))
        block = st.enter_context(nc.Block())

        @block.sync
        def _(sy):
            sq.s(sy, lambda: sy.dma_start(out=xg[:], in_=xgfm), dma=True)
            sq.s(sy, lambda: sy.dma_start(out=gw[:], in_=g2w), dma=True)
            sq.s(sy, lambda: sy.dma_start(out=gb[:], in_=g2bb), dma=True)
            sq.s(sy, lambda: sy.dma_start(out=onb[:], in_=onbc), dma=True)
        for tile in range(PC // 128):
            r = slice(tile * 128, (tile + 1) * 128)

            @block.sync
            def _(sy, r=r):
                sq.s(sy, lambda: sy.dma_start(out=oa[:], in_=of_t[r, :]), dma=True)
                sq.s(sy, lambda: sy.dma_start(out=ob[:], in_=os_t[r, :]), dma=True)
                sq.s(sy, lambda: sy.dma_start(out=lam[:], in_=lam_t[r, :]), dma=True)

            @block.vector
            def _(v):
                sq.s(v, lambda: v.tensor_sub(oa[:], oa[:], ob[:]))
                for h in range(H):
                    sq.s(v, lambda h=h: v.tensor_scalar(
                        oa[:, h * DV:(h + 1) * DV], oa[:, h * DV:(h + 1) * DV],
                        lam[:, h:h + 1], None, ALU.mult))
                sq.s(v, lambda: v.tensor_add(oa[:], oa[:], ob[:]))
            for og in range(4):
                @block.tensor
                def _(t, og=og, tile=tile):
                    sq.s(t, lambda: t.matmul(
                        ps[:, 0:512], xg[:, tile * 128:(tile + 1) * 128],
                        gw[:, og * 512:(og + 1) * 512], start=True, stop=True))

                @block.vector
                def _(v, og=og):
                    sq.s(v, lambda: v.tensor_add(
                        sig[:, og * 512:(og + 1) * 512], ps[:, 0:512],
                        gb[:, og * 512:(og + 1) * 512]))

            @block.scalar
            def _(s):
                sq.s(s, lambda: s.activation(sig[:], sig[:], AF.Tanh, scale=0.5))

            @block.vector
            def _(v):
                sq.s(v, lambda: v.tensor_scalar(sig[:], sig[:], 0.5, 0.5, ALU.mult, ALU.add))
            @block.scalar
            def _(s):
                sq.s(s, lambda: s.activation(ob[:], oa[:], AF.Square))

            @block.vector
            def _(v):
                for h in range(H):
                    sq.s(v, lambda h=h: v.reduce_sum(
                        co[:, h:h + 1], ob[:, h * DV:(h + 1) * DV], AX.X))
                sq.s(v, lambda: v.tensor_scalar(
                    co[:, 0:H], co[:, 0:H], 1.0 / DV, 1e-5, ALU.mult, ALU.add))

            @block.scalar
            def _(s):
                sq.s(s, lambda: s.activation(co[:, 0:H], co[:, 0:H], AF.Ln))
                sq.s(s, lambda: s.activation(co[:, 0:H], co[:, 0:H], AF.Exp, scale=-0.5))

            @block.vector
            def _(v):
                for h in range(H):
                    sq.s(v, lambda h=h: v.tensor_scalar(
                        oa[:, h * DV:(h + 1) * DV], oa[:, h * DV:(h + 1) * DV],
                        co[:, h:h + 1], None, ALU.mult))
                sq.s(v, lambda: v.tensor_mul(oa[:], oa[:], onb[:]))
                sq.s(v, lambda: v.tensor_mul(oa[:], oa[:], sig[:]))

            @block.gpsimd
            def _(g, r=r):
                sq.s(g, lambda: g.dma_start(out=of_o[r, :], in_=oa[:]), dma=True)

        @block.sync
        def _(sy):
            sq.fin(sy)
    return nc


def build_L6b():
    nc = bass.Bass("TRN2", target_bir_lowering=False)
    ofmT = nc.dram_tensor("ofmT", [D, PC], F32, kind="ExternalInput").ap()
    wo = nc.dram_tensor("wo", [D, D], F32, kind="ExternalInput").ap()
    x_t = nc.dram_tensor("x_t", [PC, D], F32, kind="ExternalInput").ap()
    y1_o = nc.dram_tensor("y1_o", [PC, D], F32, kind="ExternalOutput").ap()
    with ExitStack() as st:
        sq = Seq(nc, st)
        sb = lambda nm, shp: st.enter_context(nc.sbuf_tensor(nm, shp, F32))
        ofm = sb("ofm", [128, 16 * PC // 8])
        wsb = sb("wsb", [128, 512])
        acc = sb("acc", [128, D])
        xt = sb("xt", [128, D])
        co = sb("co", [128, 8])
        ps = st.enter_context(nc.psum_tensor("ps", [128, 512], F32))
        block = st.enter_context(nc.Block())
        for tile in range(PC // 128):
            r = slice(tile * 128, (tile + 1) * 128)

            @block.sync
            def _(sy, tile=tile, r=r):
                for kc in range(16):
                    sq.s(sy, lambda kc=kc: sy.dma_start(
                        out=ofm[:, kc * 128:(kc + 1) * 128],
                        in_=ofmT[kc * 128:(kc + 1) * 128, tile * 128:(tile + 1) * 128]),
                        dma=True)
                sq.s(sy, lambda: sy.dma_start(out=xt[:], in_=x_t[r, :]), dma=True)
            for og in range(4):
                @block.sync
                def _(sy, og=og):
                    pass
                for kc in range(16):
                    @block.sync
                    def _(sy, kc=kc, og=og):
                        sq.s(sy, lambda: sy.dma_start(
                            out=wsb[:],
                            in_=wo[kc * 128:(kc + 1) * 128, og * 512:(og + 1) * 512]),
                            dma=True)

                    @block.tensor
                    def _(t, kc=kc, og=og):
                        sq.s(t, lambda: t.matmul(
                            ps[:, 0:512], ofm[:, kc * 128:(kc + 1) * 128], wsb[:],
                            start=(kc == 0), stop=(kc == 15)))

                @block.vector
                def _(v, og=og):
                    sq.s(v, lambda: v.tensor_add(
                        acc[:, og * 512:(og + 1) * 512], ps[:, 0:512],
                        xt[:, og * 512:(og + 1) * 512]))
            @block.vector
            def _(v):
                sq.s(v, lambda: v.reduce_sum(co[:, 0:1], acc[:], AX.X))
                sq.s(v, lambda: v.tensor_scalar_mul(co[:, 0:1], co[:, 0:1], 1.0 / D))
                sq.s(v, lambda: v.tensor_scalar(
                    acc[:], acc[:], co[:, 0:1], None, ALU.subtract))

            @block.scalar
            def _(s):
                sq.s(s, lambda: s.activation(xt[:], acc[:], AF.Square))

            @block.vector
            def _(v):
                sq.s(v, lambda: v.reduce_sum(co[:, 1:2], xt[:], AX.X))
                sq.s(v, lambda: v.tensor_scalar(
                    co[:, 1:2], co[:, 1:2], 1.0 / D, 1e-5, ALU.mult, ALU.add))

            @block.scalar
            def _(s):
                sq.s(s, lambda: s.activation(co[:, 1:2], co[:, 1:2], AF.Ln))
                sq.s(s, lambda: s.activation(co[:, 1:2], co[:, 1:2], AF.Exp, scale=-0.5))

            @block.vector
            def _(v):
                sq.s(v, lambda: v.tensor_scalar(
                    acc[:], acc[:], co[:, 1:2], None, ALU.mult))

            @block.gpsimd
            def _(g, r=r):
                sq.s(g, lambda: g.dma_start(out=y1_o[r, :], in_=acc[:]), dma=True)

        @block.sync
        def _(sy):
            sq.fin(sy)
    return nc


def build_L7():
    nc = bass.Bass("TRN2", target_bir_lowering=False)
    yfmT = nc.dram_tensor("yfmT", [D, PC], F32, kind="ExternalInput").ap()
    y_t = nc.dram_tensor("y_t", [PC, D], F32, kind="ExternalInput").ap()
    wgT = nc.dram_tensor("wgT", [D, HID], I32, kind="ExternalInput").ap()
    wuT = nc.dram_tensor("wuT", [D, HID], I32, kind="ExternalInput").ap()
    wdT = nc.dram_tensor("wdT", [HID, D], I32, kind="ExternalInput").ap()
    sg_i = nc.dram_tensor("sg_i", [128, NHC], F32, kind="ExternalInput").ap()
    su_i = nc.dram_tensor("su_i", [128, NHC], F32, kind="ExternalInput").ap()
    sd_bc = nc.dram_tensor("sd_bc", [128, D], F32, kind="ExternalInput").ap()
    ones_c = nc.dram_tensor("ones_c", [128, 128], F32, kind="ExternalInput").ap()
    out_o = nc.dram_tensor("out_o", [PC, D], F32, kind="ExternalOutput").ap()
    gu_s = nc.dram_tensor("gu_s", [HID, PC], F32, kind="Internal").ap()
    with ExitStack() as st:
        sq = Seq(nc, st)
        sb = lambda nm, shp: st.enter_context(nc.sbuf_tensor(nm, shp, F32))
        yfm = sb("yfm", [128, 16 * PC])
        wsb = sb("wsb", [128, 16 * 128])
        wsb2 = sb("wsb2", [128, 16 * 128])
        rsb = sb("rsb", [128, PC])
        gatesb = sb("gatesb", [128, PC])
        upsb = sb("upsb", [128, PC])
        ssb = sb("ssb", [128, 2 * NHC])
        sdb = sb("sdb", [128, D])
        onesb = sb("onesb", [128, 128])
        yt = sb("yt", [128, D])
        accum = sb("accum", [128, D])
        rs2 = sb("rs2", [128, 8])
        co = sb("co", [128, 8])
        ps = st.enter_context(nc.psum_tensor("ps", [128, 512], F32))
        ps2 = st.enter_context(nc.psum_tensor("ps2", [128, 512], F32))
        psr = st.enter_context(nc.psum_tensor("psr", [128, 8], F32))
        block = st.enter_context(nc.Block())

        @block.sync
        def _(sy):
            for kc in range(16):
                sq.s(sy, lambda kc=kc: sy.dma_start(
                    out=yfm[:, kc * PC:(kc + 1) * PC],
                    in_=yfmT[kc * 128:(kc + 1) * 128, :]), dma=True)
            sq.s(sy, lambda: sy.dma_start(out=ssb[:, 0:NHC], in_=sg_i), dma=True)
            sq.s(sy, lambda: sy.dma_start(out=ssb[:, NHC:2 * NHC], in_=su_i), dma=True)
            sq.s(sy, lambda: sy.dma_start(out=sdb[:], in_=sd_bc), dma=True)
            sq.s(sy, lambda: sy.dma_start(out=onesb[:], in_=ones_c), dma=True)
        for grp in range(2):
            @block.tensor
            def _(t, grp=grp):
                for kc in range(16):
                    sq.s(t, lambda kc=kc, grp=grp: t.matmul(
                        (ps if grp == 0 else ps2)[:, 0:512], onesb[:],
                        yfm[:, kc * PC + grp * 512:kc * PC + grp * 512 + 512],
                        start=(kc == 0), stop=(kc == 15)))

            @block.vector
            def _(v, grp=grp):
                sq.s(v, lambda: v.tensor_scalar_mul(
                    rsb[:, grp * 512:(grp + 1) * 512],
                    (ps if grp == 0 else ps2)[:, 0:512], -128.0))
        for hc in range(NHC):
            @block.gpsimd
            def _(g, hc=hc):
                sq.s(g, lambda: g.dma_start(
                    out=wsb[:].rearrange("p (a f) -> p a f", a=16),
                    in_=wgT[:, hc * 128:(hc + 1) * 128].rearrange(
                        "(a p) f -> p a f", p=128)), dma=True)
                sq.s(g, lambda: g.dma_start(
                    out=wsb2[:].rearrange("p (a f) -> p a f", a=16),
                    in_=wuT[:, hc * 128:(hc + 1) * 128].rearrange(
                        "(a p) f -> p a f", p=128)), dma=True)
            for grp in range(2):
                g0 = grp * 512

                @block.tensor
                def _(t, hc=hc, g0=g0, grp=grp):
                    for kc in range(16):
                        sq.s(t, lambda kc=kc: t.matmul(
                            ps[:, 0:512], wsb[:, kc * 128:(kc + 1) * 128],
                            yfm[:, kc * PC + g0:kc * PC + g0 + 512],
                            start=(kc == 0), stop=(kc == 15)))
                    for kc in range(16):
                        sq.s(t, lambda kc=kc: t.matmul(
                            ps2[:, 0:512], wsb2[:, kc * 128:(kc + 1) * 128],
                            yfm[:, kc * PC + g0:kc * PC + g0 + 512],
                            start=(kc == 0), stop=(kc == 15)))

                @block.vector
                def _(v, g0=g0):
                    sq.s(v, lambda: v.tensor_add(
                        gatesb[:, g0:g0 + 512], ps[:, 0:512], rsb[:, g0:g0 + 512]))
                    sq.s(v, lambda: v.tensor_add(
                        upsb[:, g0:g0 + 512], ps2[:, 0:512], rsb[:, g0:g0 + 512]))

            @block.scalar
            def _(s, hc=hc):
                sq.s(s, lambda: s.activation(
                    gatesb[:], gatesb[:], AF.Silu, scale=ssb[:, hc:hc + 1]))

            @block.vector
            def _(v, hc=hc):
                sq.s(v, lambda: v.tensor_scalar(
                    upsb[:], upsb[:], ssb[:, NHC + hc:NHC + hc + 1], None, ALU.mult))
                sq.s(v, lambda: v.tensor_mul(gatesb[:], gatesb[:], upsb[:]))

            @block.tensor
            def _(t, hc=hc):
                for tl in range(8):
                    sq.s(t, lambda tl=tl, hc=hc: t.matmul(
                        psr[:, tl:tl + 1], gatesb[:, tl * 128:(tl + 1) * 128],
                        onesb[:, 0:1], start=(hc == 0 and tl == 0),
                        stop=(hc == NHC - 1)))

            @block.gpsimd
            def _(g, hc=hc):
                sq.s(g, lambda: g.dma_start(
                    out=gu_s[hc * 128:(hc + 1) * 128, :], in_=gatesb[:]), dma=True)

        @block.vector
        def _(v):
            sq.s(v, lambda: v.tensor_scalar_mul(rs2[:], psr[:], 128.0))
        for tl in range(8):
            r = slice(tl * 128, (tl + 1) * 128)
            for fg in range(4):
                @block.sync
                def _(sy, fg=fg):
                    pass
                for hc in range(NHC):
                    @block.gpsimd
                    def _(g, hc=hc, fg=fg):
                        if fg == 0 and True:
                            sq.s(g, lambda: g.dma_start(
                                out=wsb[:, 0:128], in_=gu_s[hc * 128:(hc + 1) * 128,
                                                           tl * 128:(tl + 1) * 128]),
                                dma=True)
                        sq.s(g, lambda: g.dma_start(
                            out=wsb2[:, 0:512],
                            in_=wdT[hc * 128:(hc + 1) * 128, fg * 512:(fg + 1) * 512]),
                            dma=True)
                        if fg != 0:
                            sq.s(g, lambda: g.dma_start(
                                out=wsb[:, 0:128], in_=gu_s[hc * 128:(hc + 1) * 128,
                                                            tl * 128:(tl + 1) * 128]),
                                dma=True)

                    @block.tensor
                    def _(t, hc=hc, fg=fg):
                        sq.s(t, lambda: t.matmul(
                            ps[:, 0:512], wsb[:, 0:128], wsb2[:, 0:512],
                            start=(hc == 0), stop=(hc == NHC - 1)))

                @block.vector
                def _(v, fg=fg, tl=tl):
                    sq.s(v, lambda: v.tensor_scalar(
                        accum[:, fg * 512:(fg + 1) * 512], ps[:, 0:512],
                        rs2[:, tl:tl + 1], None, ALU.subtract))
                    sq.s(v, lambda: v.tensor_mul(
                        accum[:, fg * 512:(fg + 1) * 512],
                        accum[:, fg * 512:(fg + 1) * 512],
                        sdb[:, fg * 512:(fg + 1) * 512]))

            @block.sync
            def _(sy, r=r):
                sq.s(sy, lambda: sy.dma_start(out=yt[:], in_=y_t[r, :]), dma=True)

            @block.vector
            def _(v):
                sq.s(v, lambda: v.tensor_add(accum[:], accum[:], yt[:]))
                sq.s(v, lambda: v.reduce_sum(co[:, 0:1], accum[:], AX.X))
                sq.s(v, lambda: v.tensor_scalar_mul(co[:, 0:1], co[:, 0:1], 1.0 / D))
                sq.s(v, lambda: v.tensor_scalar(
                    accum[:], accum[:], co[:, 0:1], None, ALU.subtract))

            @block.scalar
            def _(s):
                sq.s(s, lambda: s.activation(yt[:], accum[:], AF.Square))

            @block.vector
            def _(v):
                sq.s(v, lambda: v.reduce_sum(co[:, 1:2], yt[:], AX.X))
                sq.s(v, lambda: v.tensor_scalar(
                    co[:, 1:2], co[:, 1:2], 1.0 / D, 1e-5, ALU.mult, ALU.add))

            @block.scalar
            def _(s):
                sq.s(s, lambda: s.activation(co[:, 1:2], co[:, 1:2], AF.Ln))
                sq.s(s, lambda: s.activation(co[:, 1:2], co[:, 1:2], AF.Exp, scale=-0.5))

            @block.vector
            def _(v):
                sq.s(v, lambda: v.tensor_scalar(
                    accum[:], accum[:], co[:, 1:2], None, ALU.mult))

            @block.gpsimd
            def _(g, r=r):
                sq.s(g, lambda: g.dma_start(out=out_o[r, :], in_=accum[:]), dma=True)

        @block.sync
        def _(sy):
            sq.fin(sy)
    return nc


def _masks():
    am = np.zeros((10, 128, 128), np.float32)
    cm = np.zeros((10, 128, 128), np.float32)
    jj, tt = np.meshgrid(np.arange(128), np.arange(128), indexing="ij")
    for li, l in enumerate((64, 32, 16, 8)):
        same = (jj // l) == (tt // l)
        am[li] = (same & (jj <= tt)).astype(np.float32)
        am[4 + li] = (same & (jj > tt)).astype(np.float32)
        half = l
        same2 = (jj // (2 * l)) == (tt // (2 * l))
        band = same2 & ((jj // l) % 2 == 0) & ((tt // l) % 2 == 1)
        cm[li] = -band.astype(np.float32)
        cm[5 + li] = band.astype(np.float32)
    am[8] = (jj <= tt).astype(np.float32)
    am[9] = (jj > tt).astype(np.float32)
    same8 = (jj // 8) == (tt // 8)
    cm[4] = -(same8 & (jj < tt)).astype(np.float32)
    cm[9] = (same8 & (jj <= tt)).astype(np.float32)
    return (np.ascontiguousarray(am.reshape(10 * 128, 128)),
            np.ascontiguousarray(cm.reshape(10 * 128, 128)))


def _run(nc, maps):
    res = run_bass_kernel_spmd(nc, maps, core_ids=list(range(8)))
    return res.results


def kernel(**ins):
    f = lambda k: np.ascontiguousarray(np.asarray(ins[k], np.float32))
    x = f("L0_attn_hidden_states").reshape(BT, D)
    xb = x.reshape(B, T, D)
    xpad = np.concatenate([np.zeros((B, HLO, D), np.float32), xb], axis=1)
    cw = np.ascontiguousarray(np.concatenate(
        [f("L0_attn_conv_q_w"), f("L0_attn_conv_k_w"), f("L0_attn_conv_v_w")], 0))
    nc1 = build_L1()
    maps = []
    for c in range(8):
        b, t0 = c // 4, (c % 4) * PC
        win = np.ascontiguousarray(xpad[b, t0:t0 + PC + HLO, :].T)
        maps.append(dict(xT=win, wq=f("L0_attn_wq"), wk=f("L0_attn_wk"),
                         wv=f("L0_attn_wv"), cw=cw, unc=f("L0_attn_unc_w"),
                         g1=f("L0_attn_g1_w"), prox=f("L0_attn_proxy_w")))
    r1 = _run(nc1, maps)
    QS = np.concatenate([r["qs_o"] for r in r1], 1)
    K_ = np.concatenate([r["k_o"] for r in r1], 1)
    V_ = np.concatenate([r["v_o"] for r in r1], 1)
    VH = np.concatenate([r["vh_o"] for r in r1], 1)
    LG = np.concatenate([r["lg_o"] for r in r1], 1)
    XG = np.concatenate([r["xg_o"] for r in r1], 1)
    nc3 = build_L3()
    maps = []
    for c in range(8):
        sl = slice(c * PC, (c + 1) * PC)
        maps.append(dict(k_t=np.ascontiguousarray(K_[:, sl].T),
                         v_t=np.ascontiguousarray(V_[:, sl].T),
                         vh_t=np.ascontiguousarray(VH[:, sl].T),
                         lg_t=np.ascontiguousarray(LG[:, sl].T)))
    r3 = _run(nc3, maps)
    KU = np.concatenate([r["ku_o"] for r in r3], 0)
    F4 = np.concatenate([r["f4_o"] for r in r3], 0)
    ENT = np.concatenate([r["ent_o"] for r in r3], 0)[:, 0]
    he = f("L0_attn_head_embed")
    f8 = np.zeros((8, H * BT), np.float32)
    for h in range(H):
        sl = slice(h * BT, (h + 1) * BT)
        f8[0, sl] = ENT
        f8[1, sl] = F4[:, h]
        f8[2, sl] = F4[:, 16 + h]
        f8[3, sl] = F4[:, 32 + h]
        for j in range(4):
            f8[4 + j, sl] = he[h, j]
    wst = np.ascontiguousarray(np.stack(
        [f("L0_attn_bb_w"), f("L0_attn_bd_w"), f("L0_attn_lam_w"),
         f("L0_attn_ab_w"), f("L0_attn_ad_w")], 1))
    b5 = np.array([[f("L0_attn_bb_b")[0]], [f("L0_attn_bd_b")[0]],
                   [f("L0_attn_lam_b")[0]], [f("L0_attn_ab_b")[0]],
                   [f("L0_attn_ad_b")[0]]], np.float32)
    cmb = np.ascontiguousarray(np.array(
        [[1, 1, 0, 0, 0], [1, -1, 0, 0, 0], [0, 0, 1, 0, 0],
         [0, 0, 0, 1, 1], [0, 0, 0, 1, -1]], np.float32).T)
    nc4 = build_L4()
    FW = H * BT // 8
    maps = []
    for c in range(8):
        maps.append(dict(f8=np.ascontiguousarray(f8[:, c * FW:(c + 1) * FW]),
                         w1=f("L0_attn_mlp_w1"),
                         b1=f("L0_attn_mlp_b1").reshape(32, 1),
                         w2=f("L0_attn_mlp_w2"),
                         b2=f("L0_attn_mlp_b2").reshape(32, 1),
                         wst=wst, b5=b5, cmb=cmb))
    r4 = _run(nc4, maps)
    SG = np.concatenate([r["sg_o"] for r in r4], 1).reshape(5, H, BT)
    am, cm = _masks()
    ones_t = np.ones((128, 128), np.float32)
    ident = np.eye(128, dtype=np.float32)
    dtb = f("L0_attn_dt_bias").reshape(H, 128)
    alog = f("L0_attn_A_log")
    nc5 = build_L5()
    maps = []
    for c in range(8):
        kin = np.zeros((512, T), np.float32)
        qin = np.zeros((512, T), np.float32)
        vin = np.zeros((512, T), np.float32)
        ampc = np.zeros((128, 8 * NCH), np.float32)
        betc = np.zeros((128, 8 * NCH), np.float32)
        for hl in range(2):
            hg = 2 * c + hl
            for b in range(2):
                s_ = hl * 2 + b
                kin[s_ * 128:(s_ + 1) * 128] = KU.T[hg * DK:(hg + 1) * DK,
                                                    b * T:(b + 1) * T]
                qin[s_ * 128:(s_ + 1) * 128] = QS[hg * DK:(hg + 1) * DK,
                                                  b * T:(b + 1) * T]
                vin[s_ * 128:(s_ + 1) * 128] = V_[hg * DV:(hg + 1) * DV,
                                                  b * T:(b + 1) * T]
                for fs in range(2):
                    inst = s_ * 2 + fs
                    av = SG[3 + fs, hg, b * T:(b + 1) * T]
                    bv = SG[0 + fs, hg, b * T:(b + 1) * T]
                    ampc[:, inst * NCH:(inst + 1) * NCH] = av.reshape(NCH, 128).T
                    betc[:, inst * NCH:(inst + 1) * NCH] = bv.reshape(NCH, 128).T
        alogc = np.broadcast_to(alog[2 * c:2 * c + 2][None, :], (128, 2))
        maps.append(dict(kin=kin, qin=qin, vin=vin, ampc=ampc, betc=betc,
                         alogc=np.ascontiguousarray(alogc),
                         dtbc=np.ascontiguousarray(dtb[2 * c:2 * c + 2].T),
                         am=am, cm=cm, onest=ones_t, ident=ident))
    r5 = _run(nc5, maps)
    OF = np.zeros((BT, H, DV), np.float32)
    OS = np.zeros((BT, H, DV), np.float32)
    for c in range(8):
        oo = r5[c]["o_o"]
        for hl in range(2):
            for b in range(2):
                s_ = hl * 2 + b
                sl = slice(b * T, (b + 1) * T)
                OF[sl, 2 * c + hl] = oo[(s_ * 2) * T:(s_ * 2 + 1) * T]
                OS[sl, 2 * c + hl] = oo[(s_ * 2 + 1) * T:(s_ * 2 + 2) * T]
    LAM = np.ascontiguousarray(SG[2].T)
    onbc = np.broadcast_to(np.tile(f("L0_attn_onorm_w"), H)[None, :], (128, D))
    g2bb = np.broadcast_to(f("L0_attn_g2_b")[None, :], (128, D))
    nc6a = build_L6a()
    maps = []
    for c in range(8):
        sl = slice(c * PC, (c + 1) * PC)
        maps.append(dict(of_t=np.ascontiguousarray(OF.reshape(BT, D)[sl]),
                         os_t=np.ascontiguousarray(OS.reshape(BT, D)[sl]),
                         lam_t=np.ascontiguousarray(LAM[sl]),
                         xgfm=np.ascontiguousarray(XG[:, sl]),
                         g2w=f("L0_attn_g2_w"),
                         g2bb=np.ascontiguousarray(g2bb),
                         onbc=np.ascontiguousarray(onbc)))
    r6a = _run(nc6a, maps)
    OFIN = np.concatenate([r["of_o"] for r in r6a], 0)
    nc6b = build_L6b()
    maps = []
    for c in range(8):
        sl = slice(c * PC, (c + 1) * PC)
        maps.append(dict(ofmT=np.ascontiguousarray(OFIN[sl].T),
                         wo=f("L0_attn_wo"),
                         x_t=np.ascontiguousarray(x[sl])))
    r6b = _run(nc6b, maps)
    Y1 = np.concatenate([r["y1_o"] for r in r6b], 0)
    wgT = np.ascontiguousarray(np.asarray(ins["L0_ffn_wq_gate"], np.int32).T)
    wuT = np.ascontiguousarray(np.asarray(ins["L0_ffn_wq_up"], np.int32).T)
    wdT = np.ascontiguousarray(np.asarray(ins["L0_ffn_wq_down"], np.int32).T)
    sg_i = np.ascontiguousarray(f("L0_ffn_s_gate").reshape(NHC, 128).T)
    su_i = np.ascontiguousarray(f("L0_ffn_s_up").reshape(NHC, 128).T)
    sd_bc = np.ascontiguousarray(np.broadcast_to(
        f("L0_ffn_s_down")[None, :], (128, D)))
    nc7 = build_L7()
    maps = []
    for c in range(8):
        sl = slice(c * PC, (c + 1) * PC)
        maps.append(dict(yfmT=np.ascontiguousarray(Y1[sl].T),
                         y_t=np.ascontiguousarray(Y1[sl]),
                         wgT=wgT, wuT=wuT, wdT=wdT,
                         sg_i=sg_i, su_i=su_i, sd_bc=sd_bc,
                         ones_c=np.ones((128, 128), np.float32)))
    r7 = _run(nc7, maps)
    OUT = np.concatenate([r["out_o"] for r in r7], 0)
    return OUT.reshape(B, T, D)


# version 04 (final)
# speedup vs baseline: 1.0529x; 1.0529x over previous
from contextlib import ExitStack

import numpy as np
import concourse.bass as bass
import concourse.mybir as mybir
from concourse.bass_utils import run_bass_kernel_spmd

F32 = mybir.dt.float32
I32 = mybir.dt.int32
AF = mybir.ActivationFunctionType
ALU = mybir.AluOpType
AX = mybir.AxisListType

B, T, D, H, DK, DV = 2, 4096, 2048, 16, 128, 128
BT = B * T
PC = BT // 8
HLO = 16
CH = 128
NCH = T // CH
M_CL = 85.0
HID = 5632
NHC = HID // 128


class Seq:
    def __init__(self, nc, st):
        self.sem = st.enter_context(nc.semaphore("seq"))
        self.cnt = 0

    def s(self, eng, make, dma=False):
        if self.cnt:
            eng.wait_ge(self.sem, self.cnt)
        make().then_inc(self.sem, 16 if dma else 1)
        self.cnt += 16 if dma else 1

    def fin(self, eng):
        eng.wait_ge(self.sem, self.cnt)


def build_L1():
    nc = bass.Bass("TRN2", target_bir_lowering=False)
    TW = PC + HLO
    xT = nc.dram_tensor("xT", [D, TW], F32, kind="ExternalInput").ap()
    wq = nc.dram_tensor("wq", [D, D], F32, kind="ExternalInput").ap()
    wk = nc.dram_tensor("wk", [D, D], F32, kind="ExternalInput").ap()
    wv = nc.dram_tensor("wv", [D, D], F32, kind="ExternalInput").ap()
    cw = nc.dram_tensor("cw", [3 * D, 4], F32, kind="ExternalInput").ap()
    unc = nc.dram_tensor("unc", [D, 64], F32, kind="ExternalInput").ap()
    g1 = nc.dram_tensor("g1", [D, DV], F32, kind="ExternalInput").ap()
    prox = nc.dram_tensor("prox", [DK, DV], F32, kind="ExternalInput").ap()
    outs = {}
    for nm, rows in (("qs_o", D), ("k_o", D), ("v_o", D), ("vh_o", D),
                     ("lg_o", 64), ("xg_o", DV)):
        outs[nm] = nc.dram_tensor(nm, [rows, PC], F32, kind="ExternalOutput").ap()
    with ExitStack() as st:
        sq = Seq(nc, st)
        xs = st.enter_context(nc.sbuf_tensor("xs", [128, 16 * TW], F32))
        ws = st.enter_context(nc.sbuf_tensor("ws", [128, 2048], F32))
        cwsb = st.enter_context(nc.sbuf_tensor("cwsb", [128, 4], F32))
        cbuf = st.enter_context(nc.sbuf_tensor("cbuf", [128, TW], F32))
        obuf = st.enter_context(nc.sbuf_tensor("obuf", [128, TW], F32))
        ps = st.enter_context(nc.psum_tensor("ps", [128, 512], F32))
        ps2 = st.enter_context(nc.psum_tensor("ps2", [128, 512], F32))
        block = st.enter_context(nc.Block())
        NT = [(0, 512), (512, 512), (1024, TW - 1024)]

        @block.sync
        def _(sy):
            for kc in range(16):
                sq.s(sy, lambda kc=kc: sy.dma_start(
                    out=xs[:, kc * TW:(kc + 1) * TW],
                    in_=xT[kc * 128:(kc + 1) * 128, :]), dma=True)

        def proj(w_ap, conv_row, out_ap, scale, width=128, oc_n=16, tok0=HLO):
            for oc in range(oc_n):
                @block.sync
                def _(sy, oc=oc):
                    sq.s(sy, lambda: sy.dma_start(
                        out=ws[:, 0:16 * width].rearrange(
                            "p (a f) -> p a f", a=16),
                        in_=w_ap[:, oc * width:oc * width + width].rearrange(
                            "(a p) f -> p a f", p=128)), dma=True)
                    if conv_row is not None:
                        sq.s(sy, lambda: sy.dma_start(
                            out=cwsb[:, :],
                            in_=cw[conv_row + oc * 128:conv_row + (oc + 1) * 128, :]),
                            dma=True)

                for nt, (o0, nn) in enumerate(NT):
                    @block.tensor
                    def _(t, oc=oc, o0=o0, nn=nn, nt=nt):
                        for kc in range(16):
                            sq.s(t, lambda kc=kc: t.matmul(
                                (ps if nt % 2 == 0 else ps2)[0:width, 0:nn],
                                ws[:, kc * width:kc * width + width],
                                xs[:, kc * TW + o0:kc * TW + o0 + nn],
                                start=(kc == 0), stop=(kc == 15)))

                    @block.vector
                    def _(v, oc=oc, o0=o0, nn=nn, nt=nt):
                        sq.s(v, lambda: v.tensor_copy(
                            cbuf[0:width, o0:o0 + nn],
                            (ps if nt % 2 == 0 else ps2)[0:width, 0:nn]))

                if conv_row is None:
                    @block.gpsimd
                    def _(g, oc=oc):
                        sq.s(g, lambda: g.dma_start(
                            out=out_ap[oc * width:oc * width + width, :],
                            in_=cbuf[0:width, tok0:tok0 + PC]), dma=True)
                else:
                    @block.scalar
                    def _(s, oc=oc):
                        sq.s(s, lambda: s.activation(
                            obuf[:, 3:TW], cbuf[:, 3:TW], AF.Copy,
                            scale=cwsb[:, 3:4]))

                    @block.vector
                    def _(v, oc=oc):
                        for j in (2, 1, 0):
                            sq.s(v, lambda j=j: v.scalar_tensor_tensor(
                                obuf[:, 3:TW], cbuf[:, j:TW - 3 + j],
                                cwsb[:, j:j + 1], obuf[:, 3:TW],
                                ALU.mult, ALU.add))

                    @block.scalar
                    def _(s, oc=oc):
                        sq.s(s, lambda: s.activation(
                            obuf[:, HLO:TW], obuf[:, HLO:TW], AF.Silu))
                    if scale != 1.0:
                        @block.vector
                        def _(v, oc=oc):
                            sq.s(v, lambda: v.tensor_scalar_mul(
                                obuf[:, HLO:TW], obuf[:, HLO:TW], scale))

                    @block.gpsimd
                    def _(g, oc=oc):
                        sq.s(g, lambda: g.dma_start(
                            out=out_ap[oc * 128:(oc + 1) * 128, :],
                            in_=obuf[:, HLO:TW]), dma=True)

        proj(wq, 0, outs["qs_o"], DK ** -0.5)
        proj(wk, D, outs["k_o"], 1.0)
        proj(wv, 2 * D, outs["v_o"], 1.0)
        proj(unc, None, outs["lg_o"], 1.0, width=64, oc_n=1)
        proj(g1, None, outs["xg_o"], 1.0, width=128, oc_n=1)

        @block.sync
        def _(sy):
            sq.s(sy, lambda: sy.dma_start(out=ws[:, 0:DV], in_=prox), dma=True)
        for h in range(H):
            @block.gpsimd
            def _(g, h=h):
                sq.s(g, lambda: g.dma_start(
                    out=cbuf[:, 0:PC],
                    in_=outs["k_o"][h * DK:(h + 1) * DK, :]), dma=True)

            @block.tensor
            def _(t, h=h):
                for nt in range(2):
                    sq.s(t, lambda nt=nt: t.matmul(
                        (ps if nt == 0 else ps2)[:, 0:512],
                        ws[:, 0:DV], cbuf[:, nt * 512:nt * 512 + 512],
                        start=True, stop=True))

            @block.vector
            def _(v, h=h):
                for nt in range(2):
                    sq.s(v, lambda nt=nt: v.tensor_copy(
                        obuf[:, nt * 512:nt * 512 + 512],
                        (ps if nt == 0 else ps2)[:, 0:512]))

            @block.gpsimd
            def _(g, h=h):
                sq.s(g, lambda: g.dma_start(
                    out=outs["vh_o"][h * DV:(h + 1) * DV, :],
                    in_=obuf[:, 0:PC]), dma=True)

        @block.sync
        def _(sy):
            sq.fin(sy)
    return nc


def build_L3():
    nc = bass.Bass("TRN2", target_bir_lowering=False)
    k_t = nc.dram_tensor("k_t", [PC, D], F32, kind="ExternalInput").ap()
    v_t = nc.dram_tensor("v_t", [PC, D], F32, kind="ExternalInput").ap()
    vh_t = nc.dram_tensor("vh_t", [PC, D], F32, kind="ExternalInput").ap()
    lg_t = nc.dram_tensor("lg_t", [PC, 64], F32, kind="ExternalInput").ap()
    ku_o = nc.dram_tensor("ku_o", [PC, D], F32, kind="ExternalOutput").ap()
    f4_o = nc.dram_tensor("f4_o", [PC, 48], F32, kind="ExternalOutput").ap()
    ent_o = nc.dram_tensor("ent_o", [PC, 1], F32, kind="ExternalOutput").ap()
    with ExitStack() as st:
        sq = Seq(nc, st)
        kt = st.enter_context(nc.sbuf_tensor("kt", [128, D], F32))
        vt = st.enter_context(nc.sbuf_tensor("vt", [128, D], F32))
        vht = st.enter_context(nc.sbuf_tensor("vht", [128, D], F32))
        er = st.enter_context(nc.sbuf_tensor("er", [128, D], F32))
        sr = st.enter_context(nc.sbuf_tensor("sr", [128, D], F32))
        lg = st.enter_context(nc.sbuf_tensor("lg", [128, 64], F32))
        ex = st.enter_context(nc.sbuf_tensor("ex", [128, 64], F32))
        co = st.enter_context(nc.sbuf_tensor("co", [128, 128], F32))
        ft = st.enter_context(nc.sbuf_tensor("ft", [128, 48], F32))
        block = st.enter_context(nc.Block())

        for tile in range(PC // 128):
            r = slice(tile * 128, (tile + 1) * 128)

            @block.sync
            def _(sy, r=r):
                sq.s(sy, lambda: sy.dma_start(out=kt[:], in_=k_t[r, :]), dma=True)
                sq.s(sy, lambda: sy.dma_start(out=vt[:], in_=v_t[r, :]), dma=True)
                sq.s(sy, lambda: sy.dma_start(out=vht[:], in_=vh_t[r, :]), dma=True)
                sq.s(sy, lambda: sy.dma_start(out=lg[:], in_=lg_t[r, :]), dma=True)

            @block.vector
            def _(v):
                sq.s(v, lambda: v.tensor_sub(er[:], vt[:], vht[:]))
            for src, c0 in ((kt, 0), (vt, 16), (vht, 32), (er, 48)):
                @block.scalar
                def _(s, src=src):
                    sq.s(s, lambda: s.activation(sr[:], src[:], AF.Square))

                @block.vector
                def _(v, c0=c0):
                    for h in range(H):
                        sq.s(v, lambda h=h: v.reduce_sum(
                            co[:, c0 + h:c0 + h + 1],
                            sr[:, h * DK:(h + 1) * DK], AX.X))
            @block.vector
            def _(v):
                sq.s(v, lambda: v.reduce_max(co[:, 66:67], lg[:], AX.X))
                sq.s(v, lambda: v.tensor_scalar(
                    ex[:], lg[:], co[:, 66:67], None, ALU.subtract))

            @block.scalar
            def _(s):
                sq.s(s, lambda: s.activation(
                    ex[:], ex[:], AF.Exp, accum_out=co[:, 64:65]))

            @block.vector
            def _(v):
                sq.s(v, lambda: v.tensor_mul(ex[:], ex[:], lg[:]))
                sq.s(v, lambda: v.reduce_sum(co[:, 65:66], ex[:], AX.X))
                sq.s(v, lambda: v.reciprocal(co[:, 68:69], co[:, 64:65]))

            @block.scalar
            def _(s):
                sq.s(s, lambda: s.activation(co[:, 69:70], co[:, 64:65], AF.Ln))

            @block.vector
            def _(v):
                sq.s(v, lambda: v.tensor_mul(co[:, 70:71], co[:, 65:66], co[:, 68:69]))
                sq.s(v, lambda: v.tensor_add(co[:, 69:70], co[:, 69:70], co[:, 66:67]))
                sq.s(v, lambda: v.tensor_sub(co[:, 67:68], co[:, 69:70], co[:, 70:71]))
                sq.s(v, lambda: v.tensor_scalar_mul(
                    co[:, 67:68], co[:, 67:68], float(1.0 / np.log(64.0))))

            @block.scalar
            def _(s):
                sq.s(s, lambda: s.activation(co[:, 0:64], co[:, 0:64], AF.Ln))
                sq.s(s, lambda: s.activation(co[:, 0:64], co[:, 0:64], AF.Exp, scale=0.5))
            @block.vector
            def _(v):
                sq.s(v, lambda: v.tensor_scalar_add(co[:, 96:112], co[:, 0:16], 1e-6))
                sq.s(v, lambda: v.reciprocal(co[:, 96:112], co[:, 96:112]))
                for h in range(H):
                    sq.s(v, lambda h=h: v.tensor_scalar(
                        kt[:, h * DK:(h + 1) * DK], kt[:, h * DK:(h + 1) * DK],
                        co[:, 96 + h:97 + h], None, ALU.mult))

            @block.gpsimd
            def _(g, r=r):
                sq.s(g, lambda: g.dma_start(out=ku_o[r, :], in_=kt[:]), dma=True)

            @block.vector
            def _(v):
                sq.s(v, lambda: v.tensor_scalar_add(co[:, 112:128], co[:, 16:32], 1e-6))
                sq.s(v, lambda: v.reciprocal(co[:, 112:128], co[:, 112:128]))
                sq.s(v, lambda: v.tensor_mul(ft[:, 0:16], co[:, 48:64], co[:, 112:128]))
                sq.s(v, lambda: v.tensor_mul(co[:, 68:84], co[:, 16:32], co[:, 16:32]))
                sq.s(v, lambda: v.tensor_mul(co[:, 84:100], co[:, 32:48], co[:, 32:48]))
                sq.s(v, lambda: v.tensor_add(co[:, 68:84], co[:, 68:84], co[:, 84:100]))
                sq.s(v, lambda: v.tensor_mul(co[:, 84:100], co[:, 48:64], co[:, 48:64]))
                sq.s(v, lambda: v.tensor_sub(co[:, 68:84], co[:, 68:84], co[:, 84:100]))
                sq.s(v, lambda: v.tensor_scalar_mul(co[:, 68:84], co[:, 68:84], 0.5))
                sq.s(v, lambda: v.tensor_mul(co[:, 84:100], co[:, 16:32], co[:, 32:48]))
                sq.s(v, lambda: v.tensor_scalar_add(co[:, 84:100], co[:, 84:100], 1e-6))
                sq.s(v, lambda: v.reciprocal(co[:, 84:100], co[:, 84:100]))
                sq.s(v, lambda: v.tensor_mul(ft[:, 16:32], co[:, 68:84], co[:, 84:100]))

            @block.scalar
            def _(s):
                sq.s(s, lambda: s.activation(ft[:, 32:48], co[:, 48:64], AF.Ln, bias=1.0))

            @block.gpsimd
            def _(g, r=r):
                sq.s(g, lambda: g.dma_start(out=f4_o[r, :], in_=ft[:]), dma=True)
                sq.s(g, lambda: g.dma_start(out=ent_o[r, :], in_=co[:, 67:68]), dma=True)

        @block.sync
        def _(sy):
            sq.fin(sy)
    return nc


def build_L4():
    FW = H * BT // 8
    nc = bass.Bass("TRN2", target_bir_lowering=False)
    f8 = nc.dram_tensor("f8", [8, FW], F32, kind="ExternalInput").ap()
    w1 = nc.dram_tensor("w1", [8, 32], F32, kind="ExternalInput").ap()
    b1 = nc.dram_tensor("b1", [32, 1], F32, kind="ExternalInput").ap()
    w2 = nc.dram_tensor("w2", [32, 32], F32, kind="ExternalInput").ap()
    b2 = nc.dram_tensor("b2", [32, 1], F32, kind="ExternalInput").ap()
    wst = nc.dram_tensor("wst", [32, 5], F32, kind="ExternalInput").ap()
    b5 = nc.dram_tensor("b5", [5, 1], F32, kind="ExternalInput").ap()
    cmb = nc.dram_tensor("cmb", [5, 5], F32, kind="ExternalInput").ap()
    sg_o = nc.dram_tensor("sg_o", [5, FW], F32, kind="ExternalOutput").ap()
    with ExitStack() as st:
        sq = Seq(nc, st)
        fs = st.enter_context(nc.sbuf_tensor("fs", [8, 4096], F32))
        h1 = st.enter_context(nc.sbuf_tensor("h1", [32, 4096], F32))
        hf = st.enter_context(nc.sbuf_tensor("hf", [32, 4096], F32))
        r5 = st.enter_context(nc.sbuf_tensor("r5", [5, 4096], F32))
        sg = st.enter_context(nc.sbuf_tensor("sg", [5, 4096], F32))
        wsb = st.enter_context(nc.sbuf_tensor("wsb", [32, 80], F32))
        ps = st.enter_context(nc.psum_tensor("ps", [128, 512], F32))
        block = st.enter_context(nc.Block())

        @block.sync
        def _(sy):
            sq.s(sy, lambda: sy.dma_start(out=wsb[0:8, 0:32], in_=w1), dma=True)
            sq.s(sy, lambda: sy.dma_start(out=wsb[0:32, 32:64], in_=w2), dma=True)
            sq.s(sy, lambda: sy.dma_start(out=wsb[0:32, 64:69], in_=wst), dma=True)
            sq.s(sy, lambda: sy.dma_start(out=wsb[0:5, 69:74], in_=cmb), dma=True)
            sq.s(sy, lambda: sy.dma_start(out=wsb[0:32, 74:75], in_=b1), dma=True)
            sq.s(sy, lambda: sy.dma_start(out=wsb[0:32, 75:76], in_=b2), dma=True)
            sq.s(sy, lambda: sy.dma_start(out=wsb[0:5, 76:77], in_=b5), dma=True)

        for til in range(FW // 4096):
            c0 = til * 4096

            @block.sync
            def _(sy, c0=c0):
                sq.s(sy, lambda: sy.dma_start(
                    out=fs[:], in_=f8[:, c0:c0 + 4096]), dma=True)
            for sub in range(8):
                o0 = sub * 512

                @block.tensor
                def _(t, o0=o0):
                    sq.s(t, lambda: t.matmul(
                        ps[0:32, 0:512], wsb[0:8, 0:32], fs[:, o0:o0 + 512],
                        start=True, stop=True))

                @block.scalar
                def _(s, o0=o0):
                    sq.s(s, lambda: s.activation(
                        h1[:, o0:o0 + 512], ps[0:32, 0:512], AF.Silu,
                        bias=wsb[0:32, 74:75]))

                @block.tensor
                def _(t, o0=o0):
                    sq.s(t, lambda: t.matmul(
                        ps[0:32, 0:512], wsb[0:32, 32:64], h1[:, o0:o0 + 512],
                        start=True, stop=True))

                @block.scalar
                def _(s, o0=o0):
                    sq.s(s, lambda: s.activation(
                        hf[:, o0:o0 + 512], ps[0:32, 0:512], AF.Silu,
                        bias=wsb[0:32, 75:76]))

                @block.tensor
                def _(t, o0=o0):
                    sq.s(t, lambda: t.matmul(
                        ps[0:5, 0:512], wsb[0:32, 64:69], hf[:, o0:o0 + 512],
                        start=True, stop=True))

                @block.vector
                def _(v, o0=o0):
                    sq.s(v, lambda: v.tensor_scalar(
                        r5[:, o0:o0 + 512], ps[0:5, 0:512], wsb[0:5, 76:77],
                        None, ALU.add))

                @block.tensor
                def _(t, o0=o0):
                    sq.s(t, lambda: t.matmul(
                        ps[0:5, 0:512], wsb[0:5, 69:74], r5[:, o0:o0 + 512],
                        start=True, stop=True))

                @block.scalar
                def _(s, o0=o0):
                    sq.s(s, lambda: s.activation(
                        sg[:, o0:o0 + 512], ps[0:5, 0:512], AF.Tanh, scale=0.5))

                @block.vector
                def _(v, o0=o0):
                    sq.s(v, lambda: v.tensor_scalar(
                        sg[:, o0:o0 + 512], sg[:, o0:o0 + 512], 0.5, 0.5,
                        ALU.mult, ALU.add))

            @block.gpsimd
            def _(g, c0=c0):
                sq.s(g, lambda: g.dma_start(
                    out=sg_o[:, c0:c0 + 4096], in_=sg[:]), dma=True)

        @block.sync
        def _(sy):
            sq.fin(sy)
    return nc


def build_L5():
    nc = bass.Bass("TRN2", target_bir_lowering=False)
    kin = nc.dram_tensor("kin", [512, T], F32, kind="ExternalInput").ap()
    qin = nc.dram_tensor("qin", [512, T], F32, kind="ExternalInput").ap()
    vin = nc.dram_tensor("vin", [512, T], F32, kind="ExternalInput").ap()
    ampc = nc.dram_tensor("ampc", [128, 8 * NCH], F32, kind="ExternalInput").ap()
    betc = nc.dram_tensor("betc", [128, 8 * NCH], F32, kind="ExternalInput").ap()
    alogc = nc.dram_tensor("alogc", [128, 2], F32, kind="ExternalInput").ap()
    dtbc = nc.dram_tensor("dtbc", [128, 2], F32, kind="ExternalInput").ap()
    am = nc.dram_tensor("am", [10 * 128, 128], F32, kind="ExternalInput").ap()
    cm = nc.dram_tensor("cm", [10 * 128, 128], F32, kind="ExternalInput").ap()
    onest = nc.dram_tensor("onest", [128, 128], F32, kind="ExternalInput").ap()
    ident = nc.dram_tensor("ident", [128, 128], F32, kind="ExternalInput").ap()
    o_o = nc.dram_tensor("o_o", [8 * T, 128], F32, kind="ExternalOutput").ap()
    with ExitStack() as st:
        sq = Seq(nc, st)
        sb = lambda nm, shp: st.enter_context(nc.sbuf_tensor(nm, shp, F32))
        kc_ = sb("kc_", [128, 128]); qc_ = sb("qc_", [128, 128]); vc_ = sb("vc_", [128, 128])
        ampsb = sb("ampsb", [128, 8 * NCH]); betsb = sb("betsb", [128, 8 * NCH])
        gcol = sb("gcol", [128, 2]); gpos = sb("gpos", [128, 2])
        t1 = sb("t1", [128, 2]); t2 = sb("t2", [128, 2])
        amsb = sb("amsb", [128, 10 * 128]); cmsb = sb("cmsb", [128, 10 * 128])
        onesb = sb("onesb", [128, 128]); idsb = sb("idsb", [128, 128])
        ampb = sb("ampb", [128, 128])
        Dt = sb("Dt", [128, 11 * 128])
        Mt = sb("Mt", [128, 16 * 128])
        Nt = sb("Nt", [128, 128]); Aqt = sb("Aqt", [128, 128]); tmp = sb("tmp", [128, 128])
        pw = sb("pw", [128, 13 * 128])
        zz = sb("zz", [128, 128]); osb = sb("osb", [128, 128])
        khstb = sb("khstb", [128, 128]); s0d = sb("s0d", [128, 128])
        Ssb = sb("Ssb", [128, 8 * 128])
        psa = st.enter_context(nc.psum_tensor("psa", [128, 128], F32))
        psb = st.enter_context(nc.psum_tensor("psb", [128, 128], F32))
        psc = st.enter_context(nc.psum_tensor("psc", [128, 128], F32))
        block = st.enter_context(nc.Block())

        @block.sync
        def _(sy):
            sq.s(sy, lambda: sy.dma_start(out=ampsb[:], in_=ampc), dma=True)
            sq.s(sy, lambda: sy.dma_start(out=betsb[:], in_=betc), dma=True)
            sq.s(sy, lambda: sy.dma_start(out=t1[:], in_=alogc), dma=True)
            sq.s(sy, lambda: sy.dma_start(out=t2[:], in_=dtbc), dma=True)
            sq.s(sy, lambda: sy.dma_start(out=amsb[:], in_=am.rearrange("(a p) f -> p a f", p=128)), dma=True)
            sq.s(sy, lambda: sy.dma_start(out=cmsb[:], in_=cm.rearrange("(a p) f -> p a f", p=128)), dma=True)
            sq.s(sy, lambda: sy.dma_start(out=onesb[:], in_=onest), dma=True)
            sq.s(sy, lambda: sy.dma_start(out=idsb[:], in_=ident), dma=True)

        @block.scalar
        def _(s):
            sq.s(s, lambda: s.activation(gcol[:], t2[:], AF.Abs))
            sq.s(s, lambda: s.activation(gcol[:], gcol[:], AF.Exp, scale=-1.0))
            sq.s(s, lambda: s.activation(gcol[:], gcol[:], AF.Ln, bias=1.0))

        @block.vector
        def _(v):
            sq.s(v, lambda: v.tensor_relu(gpos[:], t2[:]))
            sq.s(v, lambda: v.tensor_add(gpos[:], gpos[:], gcol[:]))

        @block.scalar
        def _(s):
            sq.s(s, lambda: s.activation(t1[:], t1[:], AF.Exp))

        @block.vector
        def _(v):
            sq.s(v, lambda: v.tensor_mul(gpos[:], gpos[:], t1[:]))
            sq.s(v, lambda: v.tensor_scalar_mul(gcol[:], gpos[:], -1.0))
            sq.s(v, lambda: v.memset(Ssb[:], 0.0))

        for c in range(NCH):
            for s_ in range(4):
                hl = s_ // 2

                @block.sync
                def _(sy, c=c, s_=s_):
                    sq.s(sy, lambda: sy.dma_start(
                        out=kc_[:], in_=kin[s_ * 128:(s_ + 1) * 128, c * CH:(c + 1) * CH]), dma=True)
                    sq.s(sy, lambda: sy.dma_start(
                        out=qc_[:], in_=qin[s_ * 128:(s_ + 1) * 128, c * CH:(c + 1) * CH]), dma=True)
                    sq.s(sy, lambda: sy.dma_start(
                        out=vc_[:], in_=vin[s_ * 128:(s_ + 1) * 128, c * CH:(c + 1) * CH]), dma=True)
                for fs in range(2):
                    inst = s_ * 2 + fs
                    acol = ampsb[:, inst * NCH + c:inst * NCH + c + 1]
                    bcol = betsb[:, inst * NCH + c:inst * NCH + c + 1]
                    gc = gcol[:, hl:hl + 1]
                    gp = gpos[:, hl:hl + 1]
                    S = Ssb[:, inst * 128:(inst + 1) * 128]

                    @block.scalar
                    def _(s, acol=acol):
                        sq.s(s, lambda: s.activation(ampb[:], onesb[:], AF.Copy, scale=acol))
                    for ai in range(10):
                        @block.tensor
                        def _(t, ai=ai):
                            sq.s(t, lambda: t.matmul(
                                (psa if ai % 2 == 0 else psb)[:],
                                ampb[:], amsb[:, ai * 128:(ai + 1) * 128],
                                start=True, stop=True))

                        @block.scalar
                        def _(s, ai=ai, gc=gc):
                            sq.s(s, lambda: s.activation(
                                Dt[:, ai * 128:(ai + 1) * 128],
                                (psa if ai % 2 == 0 else psb)[:], AF.Exp, scale=gc))
                        if ai == 3:
                            @block.vector
                            def _(v, gp=gp):
                                sq.s(v, lambda: v.tensor_scalar(
                                    tmp[:], psb[:], gp, M_CL, ALU.mult, ALU.min))

                            @block.scalar
                            def _(s):
                                sq.s(s, lambda: s.activation(
                                    Dt[:, 10 * 128:11 * 128], tmp[:], AF.Exp))
                    @block.vector
                    def _(v):
                        for l in range(4):
                            sq.s(v, lambda l=l: v.scalar_tensor_tensor(
                                Mt[:, l * 128:(l + 1) * 128], kc_[:], -1.0,
                                Dt[:, l * 128:(l + 1) * 128], ALU.mult, ALU.mult))
                            sq.s(v, lambda l=l: v.tensor_mul(
                                Mt[:, (4 + l) * 128:(5 + l) * 128], qc_[:],
                                Dt[:, l * 128:(l + 1) * 128]))
                            sq.s(v, lambda l=l: v.tensor_mul(
                                Mt[:, (8 + l) * 128:(9 + l) * 128], kc_[:],
                                Dt[:, (4 + l) * 128:(5 + l) * 128]))
                        sq.s(v, lambda: v.tensor_mul(
                            Mt[:, 12 * 128:13 * 128], kc_[:], Dt[:, 10 * 128:11 * 128]))
                        sq.s(v, lambda: v.scalar_tensor_tensor(
                            Mt[:, 13 * 128:14 * 128], kc_[:], -1.0,
                            Dt[:, 8 * 128:9 * 128], ALU.mult, ALU.mult))
                        sq.s(v, lambda: v.tensor_mul(
                            Mt[:, 14 * 128:15 * 128], qc_[:], Dt[:, 8 * 128:9 * 128]))
                        sq.s(v, lambda: v.tensor_mul(
                            Mt[:, 15 * 128:16 * 128], kc_[:], Dt[:, 9 * 128:10 * 128]))
                    for which, dst in ((0, Nt), (1, Aqt)):
                        for l in range(5):
                            lhs = Mt[:, (8 + l) * 128:(9 + l) * 128] if l < 4 else Mt[:, 12 * 128:13 * 128]
                            rl = l if l < 4 else 3
                            rhs = (Mt[:, rl * 128:(rl + 1) * 128] if which == 0
                                   else Mt[:, (4 + rl) * 128:(5 + rl) * 128])
                            mi = which * 5 + l

                            @block.tensor
                            def _(t, lhs=lhs, rhs=rhs, l=l):
                                sq.s(t, lambda: t.matmul(
                                    (psa if l % 2 == 0 else psb)[:], lhs, rhs,
                                    start=True, stop=True))

                            @block.vector
                            def _(v, l=l, mi=mi, dst=dst, bcol=bcol):
                                if l == 0:
                                    sq.s(v, lambda: v.scalar_tensor_tensor(
                                        dst[:], (psa if l % 2 == 0 else psb)[:], bcol,
                                        cmsb[:, mi * 128:(mi + 1) * 128],
                                        ALU.mult, ALU.mult))
                                else:
                                    sq.s(v, lambda: v.scalar_tensor_tensor(
                                        tmp[:], (psa if l % 2 == 0 else psb)[:], bcol,
                                        cmsb[:, mi * 128:(mi + 1) * 128],
                                        ALU.mult, ALU.mult))
                                    sq.s(v, lambda: v.tensor_add(dst[:], dst[:], tmp[:]))
                    @block.tensor
                    def _(t):
                        sq.s(t, lambda: t.transpose(psc[:], Nt[:], idsb[:]))

                    @block.vector
                    def _(v):
                        sq.s(v, lambda: v.tensor_copy(pw[:, 7 * 128:8 * 128], psc[:]))
                        sq.s(v, lambda: v.tensor_copy(pw[:, 0:128], Nt[:]))
                    for kp in range(1, 7):
                        @block.tensor
                        def _(t, kp=kp):
                            sq.s(t, lambda: t.matmul(
                                psa[:], pw[:, (6 + kp) * 128:(7 + kp) * 128],
                                pw[:, (kp - 1) * 128:kp * 128], start=True, stop=True))

                        @block.vector
                        def _(v, kp=kp):
                            sq.s(v, lambda: v.tensor_copy(
                                pw[:, kp * 128:(kp + 1) * 128], psa[:]))
                        if kp < 6:
                            @block.tensor
                            def _(t, kp=kp):
                                sq.s(t, lambda: t.matmul(
                                    psb[:], pw[:, (kp - 1) * 128:kp * 128],
                                    pw[:, (6 + kp) * 128:(7 + kp) * 128],
                                    start=True, stop=True))

                            @block.vector
                            def _(v, kp=kp):
                                sq.s(v, lambda: v.tensor_copy(
                                    pw[:, (7 + kp) * 128:(8 + kp) * 128], psb[:]))
                    @block.tensor
                    def _(t):
                        sq.s(t, lambda: t.transpose(psc[:], vc_[:], idsb[:]))
                        sq.s(t, lambda: t.matmul(psc[:], Mt[:, 13 * 128:14 * 128],
                                                 S, start=False, stop=True))

                    @block.vector
                    def _(v):
                        sq.s(v, lambda: v.tensor_copy(zz[:], psc[:]))
                    for kp in (6, 5, 4, 3, 2, 1):
                        @block.tensor
                        def _(t, kp=kp):
                            sq.s(t, lambda: t.matmul(
                                psa[:], pw[:, kp * 128:(kp + 1) * 128], zz[:],
                                start=True, stop=True))

                        @block.vector
                        def _(v):
                            sq.s(v, lambda: v.tensor_add(zz[:], zz[:], psa[:]))

                    @block.tensor
                    def _(t):
                        sq.s(t, lambda: t.matmul(psa[:], pw[:, 0:128], zz[:],
                                                 start=True, stop=True))

                    @block.vector
                    def _(v):
                        sq.s(v, lambda: v.tensor_sub(zz[:], zz[:], psa[:]))
                    @block.tensor
                    def _(t):
                        sq.s(t, lambda: t.matmul(psb[:], Mt[:, 14 * 128:15 * 128], S,
                                                 start=True, stop=False))
                        sq.s(t, lambda: t.matmul(psb[:], Aqt[:], zz[:],
                                                 start=False, stop=True))

                    @block.vector
                    def _(v):
                        sq.s(v, lambda: v.tensor_copy(osb[:], psb[:]))

                    @block.gpsimd
                    def _(g, inst=inst, c=c):
                        sq.s(g, lambda: g.dma_start(
                            out=o_o[inst * T + c * CH:inst * T + (c + 1) * CH, :],
                            in_=osb[:]), dma=True)
                    @block.tensor
                    def _(t):
                        sq.s(t, lambda: t.transpose(psc[:], Mt[:, 15 * 128:16 * 128], idsb[:]))

                    @block.vector
                    def _(v, bcol=bcol):
                        sq.s(v, lambda: v.tensor_scalar(
                            khstb[:], psc[:], bcol, None, ALU.mult))
                        sq.s(v, lambda: v.tensor_scalar(
                            s0d[:], S, Dt[:, 8 * 128 + 127:8 * 128 + 128], None, ALU.mult))

                    @block.tensor
                    def _(t):
                        sq.s(t, lambda: t.matmul(psa[:], khstb[:], zz[:],
                                                 start=True, stop=False))
                        sq.s(t, lambda: t.matmul(psa[:], idsb[:], s0d[:],
                                                 start=False, stop=True))

                    @block.vector
                    def _(v, S=S):
                        sq.s(v, lambda: v.tensor_copy(S, psa[:]))

        @block.sync
        def _(sy):
            sq.fin(sy)
    return nc


def build_L6a():
    nc = bass.Bass("TRN2", target_bir_lowering=False)
    of_t = nc.dram_tensor("of_t", [PC, D], F32, kind="ExternalInput").ap()
    os_t = nc.dram_tensor("os_t", [PC, D], F32, kind="ExternalInput").ap()
    lam_t = nc.dram_tensor("lam_t", [PC, H], F32, kind="ExternalInput").ap()
    xgfm = nc.dram_tensor("xgfm", [128, PC], F32, kind="ExternalInput").ap()
    g2w = nc.dram_tensor("g2w", [128, D], F32, kind="ExternalInput").ap()
    g2bb = nc.dram_tensor("g2bb", [128, D], F32, kind="ExternalInput").ap()
    onbc = nc.dram_tensor("onbc", [128, D], F32, kind="ExternalInput").ap()
    of_o = nc.dram_tensor("of_o", [PC, D], F32, kind="ExternalOutput").ap()
    with ExitStack() as st:
        sq = Seq(nc, st)
        sb = lambda nm, shp: st.enter_context(nc.sbuf_tensor(nm, shp, F32))
        oa = sb("oa", [128, D]); ob = sb("ob", [128, D]); lam = sb("lam", [128, H])
        xg = sb("xg", [128, PC]); gw = sb("gw", [128, D]); gb = sb("gb", [128, D])
        onb = sb("onb", [128, D]); sig = sb("sig", [128, D]); co = sb("co", [128, 64])
        ps = st.enter_context(nc.psum_tensor("ps", [128, 512], F32))
        block = st.enter_context(nc.Block())

        @block.sync
        def _(sy):
            sq.s(sy, lambda: sy.dma_start(out=xg[:], in_=xgfm), dma=True)
            sq.s(sy, lambda: sy.dma_start(out=gw[:], in_=g2w), dma=True)
            sq.s(sy, lambda: sy.dma_start(out=gb[:], in_=g2bb), dma=True)
            sq.s(sy, lambda: sy.dma_start(out=onb[:], in_=onbc), dma=True)
        for tile in range(PC // 128):
            r = slice(tile * 128, (tile + 1) * 128)

            @block.sync
            def _(sy, r=r):
                sq.s(sy, lambda: sy.dma_start(out=oa[:], in_=of_t[r, :]), dma=True)
                sq.s(sy, lambda: sy.dma_start(out=ob[:], in_=os_t[r, :]), dma=True)
                sq.s(sy, lambda: sy.dma_start(out=lam[:], in_=lam_t[r, :]), dma=True)

            @block.vector
            def _(v):
                sq.s(v, lambda: v.tensor_sub(oa[:], oa[:], ob[:]))
                for h in range(H):
                    sq.s(v, lambda h=h: v.tensor_scalar(
                        oa[:, h * DV:(h + 1) * DV], oa[:, h * DV:(h + 1) * DV],
                        lam[:, h:h + 1], None, ALU.mult))
                sq.s(v, lambda: v.tensor_add(oa[:], oa[:], ob[:]))
            for og in range(4):
                @block.tensor
                def _(t, og=og, tile=tile):
                    sq.s(t, lambda: t.matmul(
                        ps[:, 0:512], xg[:, tile * 128:(tile + 1) * 128],
                        gw[:, og * 512:(og + 1) * 512], start=True, stop=True))

                @block.vector
                def _(v, og=og):
                    sq.s(v, lambda: v.tensor_add(
                        sig[:, og * 512:(og + 1) * 512], ps[:, 0:512],
                        gb[:, og * 512:(og + 1) * 512]))

            @block.scalar
            def _(s):
                sq.s(s, lambda: s.activation(sig[:], sig[:], AF.Tanh, scale=0.5))

            @block.vector
            def _(v):
                sq.s(v, lambda: v.tensor_scalar(sig[:], sig[:], 0.5, 0.5, ALU.mult, ALU.add))
            @block.scalar
            def _(s):
                sq.s(s, lambda: s.activation(ob[:], oa[:], AF.Square))

            @block.vector
            def _(v):
                for h in range(H):
                    sq.s(v, lambda h=h: v.reduce_sum(
                        co[:, h:h + 1], ob[:, h * DV:(h + 1) * DV], AX.X))
                sq.s(v, lambda: v.tensor_scalar(
                    co[:, 0:H], co[:, 0:H], 1.0 / DV, 1e-5, ALU.mult, ALU.add))

            @block.scalar
            def _(s):
                sq.s(s, lambda: s.activation(co[:, 0:H], co[:, 0:H], AF.Ln))
                sq.s(s, lambda: s.activation(co[:, 0:H], co[:, 0:H], AF.Exp, scale=-0.5))

            @block.vector
            def _(v):
                for h in range(H):
                    sq.s(v, lambda h=h: v.tensor_scalar(
                        oa[:, h * DV:(h + 1) * DV], oa[:, h * DV:(h + 1) * DV],
                        co[:, h:h + 1], None, ALU.mult))
                sq.s(v, lambda: v.tensor_mul(oa[:], oa[:], onb[:]))
                sq.s(v, lambda: v.tensor_mul(oa[:], oa[:], sig[:]))

            @block.gpsimd
            def _(g, r=r):
                sq.s(g, lambda: g.dma_start(out=of_o[r, :], in_=oa[:]), dma=True)

        @block.sync
        def _(sy):
            sq.fin(sy)
    return nc


def build_L6b():
    nc = bass.Bass("TRN2", target_bir_lowering=False)
    ofmT = nc.dram_tensor("ofmT", [D, PC], F32, kind="ExternalInput").ap()
    wo = nc.dram_tensor("wo", [D, D], F32, kind="ExternalInput").ap()
    x_t = nc.dram_tensor("x_t", [PC, D], F32, kind="ExternalInput").ap()
    y1_o = nc.dram_tensor("y1_o", [PC, D], F32, kind="ExternalOutput").ap()
    with ExitStack() as st:
        sq = Seq(nc, st)
        sb = lambda nm, shp: st.enter_context(nc.sbuf_tensor(nm, shp, F32))
        ofm = sb("ofm", [128, 16 * PC // 8])
        wsb = sb("wsb", [128, 16 * 512])
        acc = sb("acc", [128, D])
        xt = sb("xt", [128, D])
        co = sb("co", [128, 8])
        ps = st.enter_context(nc.psum_tensor("ps", [128, 512], F32))
        block = st.enter_context(nc.Block())
        for tile in range(PC // 128):
            r = slice(tile * 128, (tile + 1) * 128)

            @block.sync
            def _(sy, tile=tile, r=r):
                for kc in range(16):
                    sq.s(sy, lambda kc=kc: sy.dma_start(
                        out=ofm[:, kc * 128:(kc + 1) * 128],
                        in_=ofmT[kc * 128:(kc + 1) * 128, tile * 128:(tile + 1) * 128]),
                        dma=True)
                sq.s(sy, lambda: sy.dma_start(out=xt[:], in_=x_t[r, :]), dma=True)
            for og in range(4):
                @block.sync
                def _(sy, og=og):
                    sq.s(sy, lambda: sy.dma_start(
                        out=wsb[:].rearrange("p (a f) -> p a f", a=16),
                        in_=wo[:, og * 512:(og + 1) * 512].rearrange(
                            "(a p) f -> p a f", p=128)), dma=True)

                @block.tensor
                def _(t, og=og):
                    for kc in range(16):
                        sq.s(t, lambda kc=kc: t.matmul(
                            ps[:, 0:512], ofm[:, kc * 128:(kc + 1) * 128],
                            wsb[:, kc * 512:(kc + 1) * 512],
                            start=(kc == 0), stop=(kc == 15)))

                @block.vector
                def _(v, og=og):
                    sq.s(v, lambda: v.tensor_add(
                        acc[:, og * 512:(og + 1) * 512], ps[:, 0:512],
                        xt[:, og * 512:(og + 1) * 512]))
            @block.vector
            def _(v):
                sq.s(v, lambda: v.reduce_sum(co[:, 0:1], acc[:], AX.X))
                sq.s(v, lambda: v.tensor_scalar_mul(co[:, 0:1], co[:, 0:1], 1.0 / D))
                sq.s(v, lambda: v.tensor_scalar(
                    acc[:], acc[:], co[:, 0:1], None, ALU.subtract))

            @block.scalar
            def _(s):
                sq.s(s, lambda: s.activation(xt[:], acc[:], AF.Square))

            @block.vector
            def _(v):
                sq.s(v, lambda: v.reduce_sum(co[:, 1:2], xt[:], AX.X))
                sq.s(v, lambda: v.tensor_scalar(
                    co[:, 1:2], co[:, 1:2], 1.0 / D, 1e-5, ALU.mult, ALU.add))

            @block.scalar
            def _(s):
                sq.s(s, lambda: s.activation(co[:, 1:2], co[:, 1:2], AF.Ln))
                sq.s(s, lambda: s.activation(co[:, 1:2], co[:, 1:2], AF.Exp, scale=-0.5))

            @block.vector
            def _(v):
                sq.s(v, lambda: v.tensor_scalar(
                    acc[:], acc[:], co[:, 1:2], None, ALU.mult))

            @block.gpsimd
            def _(g, r=r):
                sq.s(g, lambda: g.dma_start(out=y1_o[r, :], in_=acc[:]), dma=True)

        @block.sync
        def _(sy):
            sq.fin(sy)
    return nc


def build_L7():
    nc = bass.Bass("TRN2", target_bir_lowering=False)
    yfmT = nc.dram_tensor("yfmT", [D, PC], F32, kind="ExternalInput").ap()
    y_t = nc.dram_tensor("y_t", [PC, D], F32, kind="ExternalInput").ap()
    wgT = nc.dram_tensor("wgT", [D, HID], I32, kind="ExternalInput").ap()
    wuT = nc.dram_tensor("wuT", [D, HID], I32, kind="ExternalInput").ap()
    wdT = nc.dram_tensor("wdT", [HID, D], I32, kind="ExternalInput").ap()
    sg_i = nc.dram_tensor("sg_i", [128, NHC], F32, kind="ExternalInput").ap()
    su_i = nc.dram_tensor("su_i", [128, NHC], F32, kind="ExternalInput").ap()
    sd_bc = nc.dram_tensor("sd_bc", [128, D], F32, kind="ExternalInput").ap()
    ones_c = nc.dram_tensor("ones_c", [128, 128], F32, kind="ExternalInput").ap()
    out_o = nc.dram_tensor("out_o", [PC, D], F32, kind="ExternalOutput").ap()
    gu_s = nc.dram_tensor("gu_s", [HID, PC], F32, kind="Internal").ap()
    with ExitStack() as st:
        sq = Seq(nc, st)
        sb = lambda nm, shp: st.enter_context(nc.sbuf_tensor(nm, shp, F32))
        yfm = sb("yfm", [128, 16 * PC])
        wsb = sb("wsb", [128, 16 * 128])
        wsb2 = sb("wsb2", [128, 16 * 128])
        rsb = sb("rsb", [128, PC])
        gatesb = sb("gatesb", [128, PC])
        upsb = sb("upsb", [128, PC])
        ssb = sb("ssb", [128, 2 * NHC])
        sdb = sb("sdb", [128, D])
        onesb = sb("onesb", [128, 128])
        yt = sb("yt", [128, D])
        accum = sb("accum", [128, D])
        rs2 = sb("rs2", [128, 8])
        co = sb("co", [128, 8])
        ps = st.enter_context(nc.psum_tensor("ps", [128, 512], F32))
        ps2 = st.enter_context(nc.psum_tensor("ps2", [128, 512], F32))
        psr = st.enter_context(nc.psum_tensor("psr", [128, 8], F32))
        block = st.enter_context(nc.Block())

        @block.sync
        def _(sy):
            for kc in range(16):
                sq.s(sy, lambda kc=kc: sy.dma_start(
                    out=yfm[:, kc * PC:(kc + 1) * PC],
                    in_=yfmT[kc * 128:(kc + 1) * 128, :]), dma=True)
            sq.s(sy, lambda: sy.dma_start(out=ssb[:, 0:NHC], in_=sg_i), dma=True)
            sq.s(sy, lambda: sy.dma_start(out=ssb[:, NHC:2 * NHC], in_=su_i), dma=True)
            sq.s(sy, lambda: sy.dma_start(out=sdb[:], in_=sd_bc), dma=True)
            sq.s(sy, lambda: sy.dma_start(out=onesb[:], in_=ones_c), dma=True)
        for grp in range(2):
            @block.tensor
            def _(t, grp=grp):
                for kc in range(16):
                    sq.s(t, lambda kc=kc, grp=grp: t.matmul(
                        (ps if grp == 0 else ps2)[:, 0:512], onesb[:],
                        yfm[:, kc * PC + grp * 512:kc * PC + grp * 512 + 512],
                        start=(kc == 0), stop=(kc == 15)))

            @block.vector
            def _(v, grp=grp):
                sq.s(v, lambda: v.tensor_scalar_mul(
                    rsb[:, grp * 512:(grp + 1) * 512],
                    (ps if grp == 0 else ps2)[:, 0:512], -128.0))
        for hc in range(NHC):
            @block.gpsimd
            def _(g, hc=hc):
                sq.s(g, lambda: g.dma_start(
                    out=wsb[:].rearrange("p (a f) -> p a f", a=16),
                    in_=wgT[:, hc * 128:(hc + 1) * 128].rearrange(
                        "(a p) f -> p a f", p=128)), dma=True)
                sq.s(g, lambda: g.dma_start(
                    out=wsb2[:].rearrange("p (a f) -> p a f", a=16),
                    in_=wuT[:, hc * 128:(hc + 1) * 128].rearrange(
                        "(a p) f -> p a f", p=128)), dma=True)
            for grp in range(2):
                g0 = grp * 512

                @block.tensor
                def _(t, hc=hc, g0=g0, grp=grp):
                    for kc in range(16):
                        sq.s(t, lambda kc=kc: t.matmul(
                            ps[:, 0:512], wsb[:, kc * 128:(kc + 1) * 128],
                            yfm[:, kc * PC + g0:kc * PC + g0 + 512],
                            start=(kc == 0), stop=(kc == 15)))
                    for kc in range(16):
                        sq.s(t, lambda kc=kc: t.matmul(
                            ps2[:, 0:512], wsb2[:, kc * 128:(kc + 1) * 128],
                            yfm[:, kc * PC + g0:kc * PC + g0 + 512],
                            start=(kc == 0), stop=(kc == 15)))

                @block.vector
                def _(v, g0=g0):
                    sq.s(v, lambda: v.tensor_add(
                        gatesb[:, g0:g0 + 512], ps[:, 0:512], rsb[:, g0:g0 + 512]))
                    sq.s(v, lambda: v.tensor_add(
                        upsb[:, g0:g0 + 512], ps2[:, 0:512], rsb[:, g0:g0 + 512]))

            @block.scalar
            def _(s, hc=hc):
                sq.s(s, lambda: s.activation(
                    gatesb[:], gatesb[:], AF.Silu, scale=ssb[:, hc:hc + 1]))

            @block.vector
            def _(v, hc=hc):
                sq.s(v, lambda: v.tensor_scalar(
                    upsb[:], upsb[:], ssb[:, NHC + hc:NHC + hc + 1], None, ALU.mult))
                sq.s(v, lambda: v.tensor_mul(gatesb[:], gatesb[:], upsb[:]))

            @block.tensor
            def _(t, hc=hc):
                for tl in range(8):
                    sq.s(t, lambda tl=tl, hc=hc: t.matmul(
                        psr[:, tl:tl + 1], gatesb[:, tl * 128:(tl + 1) * 128],
                        onesb[:, 0:1], start=(hc == 0 and tl == 0),
                        stop=(hc == NHC - 1)))

            @block.gpsimd
            def _(g, hc=hc):
                sq.s(g, lambda: g.dma_start(
                    out=gu_s[hc * 128:(hc + 1) * 128, :], in_=gatesb[:]), dma=True)

        @block.vector
        def _(v):
            sq.s(v, lambda: v.tensor_scalar_mul(rs2[:], psr[:], 128.0))
        for tl in range(8):
            r = slice(tl * 128, (tl + 1) * 128)
            for fg in range(4):
                @block.sync
                def _(sy, fg=fg):
                    pass
                for hc in range(NHC):
                    @block.gpsimd
                    def _(g, hc=hc, fg=fg):
                        if fg == 0 and True:
                            sq.s(g, lambda: g.dma_start(
                                out=wsb[:, 0:128], in_=gu_s[hc * 128:(hc + 1) * 128,
                                                           tl * 128:(tl + 1) * 128]),
                                dma=True)
                        sq.s(g, lambda: g.dma_start(
                            out=wsb2[:, 0:512],
                            in_=wdT[hc * 128:(hc + 1) * 128, fg * 512:(fg + 1) * 512]),
                            dma=True)
                        if fg != 0:
                            sq.s(g, lambda: g.dma_start(
                                out=wsb[:, 0:128], in_=gu_s[hc * 128:(hc + 1) * 128,
                                                            tl * 128:(tl + 1) * 128]),
                                dma=True)

                    @block.tensor
                    def _(t, hc=hc, fg=fg):
                        sq.s(t, lambda: t.matmul(
                            ps[:, 0:512], wsb[:, 0:128], wsb2[:, 0:512],
                            start=(hc == 0), stop=(hc == NHC - 1)))

                @block.vector
                def _(v, fg=fg, tl=tl):
                    sq.s(v, lambda: v.tensor_scalar(
                        accum[:, fg * 512:(fg + 1) * 512], ps[:, 0:512],
                        rs2[:, tl:tl + 1], None, ALU.subtract))
                    sq.s(v, lambda: v.tensor_mul(
                        accum[:, fg * 512:(fg + 1) * 512],
                        accum[:, fg * 512:(fg + 1) * 512],
                        sdb[:, fg * 512:(fg + 1) * 512]))

            @block.sync
            def _(sy, r=r):
                sq.s(sy, lambda: sy.dma_start(out=yt[:], in_=y_t[r, :]), dma=True)

            @block.vector
            def _(v):
                sq.s(v, lambda: v.tensor_add(accum[:], accum[:], yt[:]))
                sq.s(v, lambda: v.reduce_sum(co[:, 0:1], accum[:], AX.X))
                sq.s(v, lambda: v.tensor_scalar_mul(co[:, 0:1], co[:, 0:1], 1.0 / D))
                sq.s(v, lambda: v.tensor_scalar(
                    accum[:], accum[:], co[:, 0:1], None, ALU.subtract))

            @block.scalar
            def _(s):
                sq.s(s, lambda: s.activation(yt[:], accum[:], AF.Square))

            @block.vector
            def _(v):
                sq.s(v, lambda: v.reduce_sum(co[:, 1:2], yt[:], AX.X))
                sq.s(v, lambda: v.tensor_scalar(
                    co[:, 1:2], co[:, 1:2], 1.0 / D, 1e-5, ALU.mult, ALU.add))

            @block.scalar
            def _(s):
                sq.s(s, lambda: s.activation(co[:, 1:2], co[:, 1:2], AF.Ln))
                sq.s(s, lambda: s.activation(co[:, 1:2], co[:, 1:2], AF.Exp, scale=-0.5))

            @block.vector
            def _(v):
                sq.s(v, lambda: v.tensor_scalar(
                    accum[:], accum[:], co[:, 1:2], None, ALU.mult))

            @block.gpsimd
            def _(g, r=r):
                sq.s(g, lambda: g.dma_start(out=out_o[r, :], in_=accum[:]), dma=True)

        @block.sync
        def _(sy):
            sq.fin(sy)
    return nc


def _masks():
    am = np.zeros((10, 128, 128), np.float32)
    cm = np.zeros((10, 128, 128), np.float32)
    jj, tt = np.meshgrid(np.arange(128), np.arange(128), indexing="ij")
    for li, l in enumerate((64, 32, 16, 8)):
        same = (jj // l) == (tt // l)
        am[li] = (same & (jj <= tt)).astype(np.float32)
        am[4 + li] = (same & (jj > tt)).astype(np.float32)
        half = l
        same2 = (jj // (2 * l)) == (tt // (2 * l))
        band = same2 & ((jj // l) % 2 == 0) & ((tt // l) % 2 == 1)
        cm[li] = -band.astype(np.float32)
        cm[5 + li] = band.astype(np.float32)
    am[8] = (jj <= tt).astype(np.float32)
    am[9] = (jj > tt).astype(np.float32)
    same8 = (jj // 8) == (tt // 8)
    cm[4] = -(same8 & (jj < tt)).astype(np.float32)
    cm[9] = (same8 & (jj <= tt)).astype(np.float32)
    return (np.ascontiguousarray(am.reshape(10 * 128, 128)),
            np.ascontiguousarray(cm.reshape(10 * 128, 128)))


def _run(nc, maps):
    res = run_bass_kernel_spmd(nc, maps, core_ids=list(range(8)))
    return res.results


def kernel(**ins):
    f = lambda k: np.ascontiguousarray(np.asarray(ins[k], np.float32))
    x = f("L0_attn_hidden_states").reshape(BT, D)
    xb = x.reshape(B, T, D)
    xpad = np.concatenate([np.zeros((B, HLO, D), np.float32), xb], axis=1)
    cw = np.ascontiguousarray(np.concatenate(
        [f("L0_attn_conv_q_w"), f("L0_attn_conv_k_w"), f("L0_attn_conv_v_w")], 0))
    nc1 = build_L1()
    maps = []
    for c in range(8):
        b, t0 = c // 4, (c % 4) * PC
        win = np.ascontiguousarray(xpad[b, t0:t0 + PC + HLO, :].T)
        maps.append(dict(xT=win, wq=f("L0_attn_wq"), wk=f("L0_attn_wk"),
                         wv=f("L0_attn_wv"), cw=cw, unc=f("L0_attn_unc_w"),
                         g1=f("L0_attn_g1_w"), prox=f("L0_attn_proxy_w")))
    r1 = _run(nc1, maps)
    QS = np.concatenate([r["qs_o"] for r in r1], 1)
    K_ = np.concatenate([r["k_o"] for r in r1], 1)
    V_ = np.concatenate([r["v_o"] for r in r1], 1)
    VH = np.concatenate([r["vh_o"] for r in r1], 1)
    LG = np.concatenate([r["lg_o"] for r in r1], 1)
    XG = np.concatenate([r["xg_o"] for r in r1], 1)
    nc3 = build_L3()
    maps = []
    for c in range(8):
        sl = slice(c * PC, (c + 1) * PC)
        maps.append(dict(k_t=np.ascontiguousarray(K_[:, sl].T),
                         v_t=np.ascontiguousarray(V_[:, sl].T),
                         vh_t=np.ascontiguousarray(VH[:, sl].T),
                         lg_t=np.ascontiguousarray(LG[:, sl].T)))
    r3 = _run(nc3, maps)
    KU = np.concatenate([r["ku_o"] for r in r3], 0)
    F4 = np.concatenate([r["f4_o"] for r in r3], 0)
    ENT = np.concatenate([r["ent_o"] for r in r3], 0)[:, 0]
    he = f("L0_attn_head_embed")
    f8 = np.zeros((8, H * BT), np.float32)
    for h in range(H):
        sl = slice(h * BT, (h + 1) * BT)
        f8[0, sl] = ENT
        f8[1, sl] = F4[:, h]
        f8[2, sl] = F4[:, 16 + h]
        f8[3, sl] = F4[:, 32 + h]
        for j in range(4):
            f8[4 + j, sl] = he[h, j]
    wst = np.ascontiguousarray(np.stack(
        [f("L0_attn_bb_w"), f("L0_attn_bd_w"), f("L0_attn_lam_w"),
         f("L0_attn_ab_w"), f("L0_attn_ad_w")], 1))
    b5 = np.array([[f("L0_attn_bb_b")[0]], [f("L0_attn_bd_b")[0]],
                   [f("L0_attn_lam_b")[0]], [f("L0_attn_ab_b")[0]],
                   [f("L0_attn_ad_b")[0]]], np.float32)
    cmb = np.ascontiguousarray(np.array(
        [[1, 1, 0, 0, 0], [1, -1, 0, 0, 0], [0, 0, 1, 0, 0],
         [0, 0, 0, 1, 1], [0, 0, 0, 1, -1]], np.float32).T)
    nc4 = build_L4()
    FW = H * BT // 8
    maps = []
    for c in range(8):
        maps.append(dict(f8=np.ascontiguousarray(f8[:, c * FW:(c + 1) * FW]),
                         w1=f("L0_attn_mlp_w1"),
                         b1=f("L0_attn_mlp_b1").reshape(32, 1),
                         w2=f("L0_attn_mlp_w2"),
                         b2=f("L0_attn_mlp_b2").reshape(32, 1),
                         wst=wst, b5=b5, cmb=cmb))
    r4 = _run(nc4, maps)
    SG = np.concatenate([r["sg_o"] for r in r4], 1).reshape(5, H, BT)
    am, cm = _masks()
    ones_t = np.ones((128, 128), np.float32)
    ident = np.eye(128, dtype=np.float32)
    dtb = f("L0_attn_dt_bias").reshape(H, 128)
    alog = f("L0_attn_A_log")
    nc5 = build_L5()
    maps = []
    for c in range(8):
        kin = np.zeros((512, T), np.float32)
        qin = np.zeros((512, T), np.float32)
        vin = np.zeros((512, T), np.float32)
        ampc = np.zeros((128, 8 * NCH), np.float32)
        betc = np.zeros((128, 8 * NCH), np.float32)
        for hl in range(2):
            hg = 2 * c + hl
            for b in range(2):
                s_ = hl * 2 + b
                kin[s_ * 128:(s_ + 1) * 128] = KU.T[hg * DK:(hg + 1) * DK,
                                                    b * T:(b + 1) * T]
                qin[s_ * 128:(s_ + 1) * 128] = QS[hg * DK:(hg + 1) * DK,
                                                  b * T:(b + 1) * T]
                vin[s_ * 128:(s_ + 1) * 128] = V_[hg * DV:(hg + 1) * DV,
                                                  b * T:(b + 1) * T]
                for fs in range(2):
                    inst = s_ * 2 + fs
                    av = SG[3 + fs, hg, b * T:(b + 1) * T]
                    bv = SG[0 + fs, hg, b * T:(b + 1) * T]
                    ampc[:, inst * NCH:(inst + 1) * NCH] = av.reshape(NCH, 128).T
                    betc[:, inst * NCH:(inst + 1) * NCH] = bv.reshape(NCH, 128).T
        alogc = np.broadcast_to(alog[2 * c:2 * c + 2][None, :], (128, 2))
        maps.append(dict(kin=kin, qin=qin, vin=vin, ampc=ampc, betc=betc,
                         alogc=np.ascontiguousarray(alogc),
                         dtbc=np.ascontiguousarray(dtb[2 * c:2 * c + 2].T),
                         am=am, cm=cm, onest=ones_t, ident=ident))
    r5 = _run(nc5, maps)
    OF = np.zeros((BT, H, DV), np.float32)
    OS = np.zeros((BT, H, DV), np.float32)
    for c in range(8):
        oo = r5[c]["o_o"]
        for hl in range(2):
            for b in range(2):
                s_ = hl * 2 + b
                sl = slice(b * T, (b + 1) * T)
                OF[sl, 2 * c + hl] = oo[(s_ * 2) * T:(s_ * 2 + 1) * T]
                OS[sl, 2 * c + hl] = oo[(s_ * 2 + 1) * T:(s_ * 2 + 2) * T]
    LAM = np.ascontiguousarray(SG[2].T)
    onbc = np.broadcast_to(np.tile(f("L0_attn_onorm_w"), H)[None, :], (128, D))
    g2bb = np.broadcast_to(f("L0_attn_g2_b")[None, :], (128, D))
    nc6a = build_L6a()
    maps = []
    for c in range(8):
        sl = slice(c * PC, (c + 1) * PC)
        maps.append(dict(of_t=np.ascontiguousarray(OF.reshape(BT, D)[sl]),
                         os_t=np.ascontiguousarray(OS.reshape(BT, D)[sl]),
                         lam_t=np.ascontiguousarray(LAM[sl]),
                         xgfm=np.ascontiguousarray(XG[:, sl]),
                         g2w=f("L0_attn_g2_w"),
                         g2bb=np.ascontiguousarray(g2bb),
                         onbc=np.ascontiguousarray(onbc)))
    r6a = _run(nc6a, maps)
    OFIN = np.concatenate([r["of_o"] for r in r6a], 0)
    nc6b = build_L6b()
    maps = []
    for c in range(8):
        sl = slice(c * PC, (c + 1) * PC)
        maps.append(dict(ofmT=np.ascontiguousarray(OFIN[sl].T),
                         wo=f("L0_attn_wo"),
                         x_t=np.ascontiguousarray(x[sl])))
    r6b = _run(nc6b, maps)
    Y1 = np.concatenate([r["y1_o"] for r in r6b], 0)
    wgT = np.ascontiguousarray(np.asarray(ins["L0_ffn_wq_gate"], np.int32).T)
    wuT = np.ascontiguousarray(np.asarray(ins["L0_ffn_wq_up"], np.int32).T)
    wdT = np.ascontiguousarray(np.asarray(ins["L0_ffn_wq_down"], np.int32).T)
    sg_i = np.ascontiguousarray(f("L0_ffn_s_gate").reshape(NHC, 128).T)
    su_i = np.ascontiguousarray(f("L0_ffn_s_up").reshape(NHC, 128).T)
    sd_bc = np.ascontiguousarray(np.broadcast_to(
        f("L0_ffn_s_down")[None, :], (128, D)))
    nc7 = build_L7()
    maps = []
    for c in range(8):
        sl = slice(c * PC, (c + 1) * PC)
        maps.append(dict(yfmT=np.ascontiguousarray(Y1[sl].T),
                         y_t=np.ascontiguousarray(Y1[sl]),
                         wgT=wgT, wuT=wuT, wdT=wdT,
                         sg_i=sg_i, su_i=su_i, sd_bc=sd_bc,
                         ones_c=np.ones((128, 128), np.float32)))
    r7 = _run(nc7, maps)
    OUT = np.concatenate([r["out_o"] for r in r7], 0)
    return OUT.reshape(B, T, D)


# version 05
# speedup vs baseline: 1.0679x; 1.0143x over previous
from contextlib import ExitStack

import numpy as np
import concourse.bass as bass
import concourse.mybir as mybir
from concourse.bass_utils import run_bass_kernel_spmd

F32 = mybir.dt.float32
I32 = mybir.dt.int32
AF = mybir.ActivationFunctionType
ALU = mybir.AluOpType
AX = mybir.AxisListType

B, T, D, H, DK, DV = 2, 4096, 2048, 16, 128, 128
BT = B * T
PC = BT // 8
HLO = 16
CH = 128
NCH = T // CH
M_CL = 85.0
HID = 5632
NHC = HID // 128


class Seq:
    def __init__(self, nc, st):
        self.sem = st.enter_context(nc.semaphore("seq"))
        self.cnt = 0

    def s(self, eng, make, dma=False):
        if self.cnt:
            eng.wait_ge(self.sem, self.cnt)
        make().then_inc(self.sem, 16 if dma else 1)
        self.cnt += 16 if dma else 1

    def fin(self, eng):
        eng.wait_ge(self.sem, self.cnt)


def build_L1():
    nc = bass.Bass("TRN2", target_bir_lowering=False)
    TW = PC + HLO
    xT = nc.dram_tensor("xT", [D, TW], F32, kind="ExternalInput").ap()
    wq = nc.dram_tensor("wq", [D, D], F32, kind="ExternalInput").ap()
    wk = nc.dram_tensor("wk", [D, D], F32, kind="ExternalInput").ap()
    wv = nc.dram_tensor("wv", [D, D], F32, kind="ExternalInput").ap()
    cw = nc.dram_tensor("cw", [3 * D, 4], F32, kind="ExternalInput").ap()
    unc = nc.dram_tensor("unc", [D, 64], F32, kind="ExternalInput").ap()
    g1 = nc.dram_tensor("g1", [D, DV], F32, kind="ExternalInput").ap()
    prox = nc.dram_tensor("prox", [DK, DV], F32, kind="ExternalInput").ap()
    outs = {}
    for nm, rows in (("qs_o", D), ("k_o", D), ("v_o", D), ("vh_o", D),
                     ("lg_o", 64), ("xg_o", DV)):
        outs[nm] = nc.dram_tensor(nm, [rows, PC], F32, kind="ExternalOutput").ap()
    with ExitStack() as st:
        sq = Seq(nc, st)
        xs = st.enter_context(nc.sbuf_tensor("xs", [128, 16 * TW], F32))
        ws = st.enter_context(nc.sbuf_tensor("ws", [128, 2048], F32))
        cwsb = st.enter_context(nc.sbuf_tensor("cwsb", [128, 4], F32))
        cbuf = st.enter_context(nc.sbuf_tensor("cbuf", [128, TW], F32))
        obuf = st.enter_context(nc.sbuf_tensor("obuf", [128, TW], F32))
        ps = st.enter_context(nc.psum_tensor("ps", [128, 512], F32))
        ps2 = st.enter_context(nc.psum_tensor("ps2", [128, 512], F32))
        block = st.enter_context(nc.Block())
        NT = [(0, 512), (512, 512), (1024, TW - 1024)]

        @block.sync
        def _(sy):
            for kc in range(16):
                sq.s(sy, lambda kc=kc: sy.dma_start(
                    out=xs[:, kc * TW:(kc + 1) * TW],
                    in_=xT[kc * 128:(kc + 1) * 128, :]), dma=True)

        def proj(w_ap, conv_row, out_ap, scale, width=128, oc_n=16, tok0=HLO):
            for oc in range(oc_n):
                @block.sync
                def _(sy, oc=oc):
                    sq.s(sy, lambda: sy.dma_start(
                        out=ws[:, 0:16 * width].rearrange(
                            "p (a f) -> p a f", a=16),
                        in_=w_ap[:, oc * width:oc * width + width].rearrange(
                            "(a p) f -> p a f", p=128)), dma=True)
                    if conv_row is not None:
                        sq.s(sy, lambda: sy.dma_start(
                            out=cwsb[:, :],
                            in_=cw[conv_row + oc * 128:conv_row + (oc + 1) * 128, :]),
                            dma=True)

                for nt, (o0, nn) in enumerate(NT):
                    @block.tensor
                    def _(t, oc=oc, o0=o0, nn=nn, nt=nt):
                        for kc in range(16):
                            sq.s(t, lambda kc=kc: t.matmul(
                                (ps if nt % 2 == 0 else ps2)[0:width, 0:nn],
                                ws[:, kc * width:kc * width + width],
                                xs[:, kc * TW + o0:kc * TW + o0 + nn],
                                start=(kc == 0), stop=(kc == 15)))

                    @block.vector
                    def _(v, oc=oc, o0=o0, nn=nn, nt=nt):
                        sq.s(v, lambda: v.tensor_copy(
                            cbuf[0:width, o0:o0 + nn],
                            (ps if nt % 2 == 0 else ps2)[0:width, 0:nn]))

                if conv_row is None:
                    @block.gpsimd
                    def _(g, oc=oc):
                        sq.s(g, lambda: g.dma_start(
                            out=out_ap[oc * width:oc * width + width, :],
                            in_=cbuf[0:width, tok0:tok0 + PC]), dma=True)
                else:
                    @block.scalar
                    def _(s, oc=oc):
                        sq.s(s, lambda: s.activation(
                            obuf[:, 3:TW], cbuf[:, 3:TW], AF.Copy,
                            scale=cwsb[:, 3:4]))

                    @block.vector
                    def _(v, oc=oc):
                        for j in (2, 1, 0):
                            sq.s(v, lambda j=j: v.scalar_tensor_tensor(
                                obuf[:, 3:TW], cbuf[:, j:TW - 3 + j],
                                cwsb[:, j:j + 1], obuf[:, 3:TW],
                                ALU.mult, ALU.add))

                    @block.scalar
                    def _(s, oc=oc):
                        sq.s(s, lambda: s.activation(
                            obuf[:, HLO:TW], obuf[:, HLO:TW], AF.Silu))
                    if scale != 1.0:
                        @block.vector
                        def _(v, oc=oc):
                            sq.s(v, lambda: v.tensor_scalar_mul(
                                obuf[:, HLO:TW], obuf[:, HLO:TW], scale))

                    @block.gpsimd
                    def _(g, oc=oc):
                        sq.s(g, lambda: g.dma_start(
                            out=out_ap[oc * 128:(oc + 1) * 128, :],
                            in_=obuf[:, HLO:TW]), dma=True)

        proj(wq, 0, outs["qs_o"], DK ** -0.5)
        proj(wk, D, outs["k_o"], 1.0)
        proj(wv, 2 * D, outs["v_o"], 1.0)
        proj(unc, None, outs["lg_o"], 1.0, width=64, oc_n=1)
        proj(g1, None, outs["xg_o"], 1.0, width=128, oc_n=1)

        @block.sync
        def _(sy):
            sq.s(sy, lambda: sy.dma_start(out=ws[:, 0:DV], in_=prox), dma=True)
        for h in range(H):
            @block.gpsimd
            def _(g, h=h):
                sq.s(g, lambda: g.dma_start(
                    out=cbuf[:, 0:PC],
                    in_=outs["k_o"][h * DK:(h + 1) * DK, :]), dma=True)

            @block.tensor
            def _(t, h=h):
                for nt in range(2):
                    sq.s(t, lambda nt=nt: t.matmul(
                        (ps if nt == 0 else ps2)[:, 0:512],
                        ws[:, 0:DV], cbuf[:, nt * 512:nt * 512 + 512],
                        start=True, stop=True))

            @block.vector
            def _(v, h=h):
                for nt in range(2):
                    sq.s(v, lambda nt=nt: v.tensor_copy(
                        obuf[:, nt * 512:nt * 512 + 512],
                        (ps if nt == 0 else ps2)[:, 0:512]))

            @block.gpsimd
            def _(g, h=h):
                sq.s(g, lambda: g.dma_start(
                    out=outs["vh_o"][h * DV:(h + 1) * DV, :],
                    in_=obuf[:, 0:PC]), dma=True)

        @block.sync
        def _(sy):
            sq.fin(sy)
    return nc


def build_L3():
    nc = bass.Bass("TRN2", target_bir_lowering=False)
    k_t = nc.dram_tensor("k_t", [PC, D], F32, kind="ExternalInput").ap()
    v_t = nc.dram_tensor("v_t", [PC, D], F32, kind="ExternalInput").ap()
    vh_t = nc.dram_tensor("vh_t", [PC, D], F32, kind="ExternalInput").ap()
    lg_t = nc.dram_tensor("lg_t", [PC, 64], F32, kind="ExternalInput").ap()
    ku_o = nc.dram_tensor("ku_o", [PC, D], F32, kind="ExternalOutput").ap()
    f4_o = nc.dram_tensor("f4_o", [PC, 48], F32, kind="ExternalOutput").ap()
    ent_o = nc.dram_tensor("ent_o", [PC, 1], F32, kind="ExternalOutput").ap()
    with ExitStack() as st:
        sq = Seq(nc, st)
        kt = st.enter_context(nc.sbuf_tensor("kt", [128, D], F32))
        vt = st.enter_context(nc.sbuf_tensor("vt", [128, D], F32))
        vht = st.enter_context(nc.sbuf_tensor("vht", [128, D], F32))
        er = st.enter_context(nc.sbuf_tensor("er", [128, D], F32))
        sr = st.enter_context(nc.sbuf_tensor("sr", [128, D], F32))
        lg = st.enter_context(nc.sbuf_tensor("lg", [128, 64], F32))
        ex = st.enter_context(nc.sbuf_tensor("ex", [128, 64], F32))
        co = st.enter_context(nc.sbuf_tensor("co", [128, 128], F32))
        ft = st.enter_context(nc.sbuf_tensor("ft", [128, 48], F32))
        block = st.enter_context(nc.Block())

        for tile in range(PC // 128):
            r = slice(tile * 128, (tile + 1) * 128)

            @block.sync
            def _(sy, r=r):
                sq.s(sy, lambda: sy.dma_start(out=kt[:], in_=k_t[r, :]), dma=True)
                sq.s(sy, lambda: sy.dma_start(out=vt[:], in_=v_t[r, :]), dma=True)
                sq.s(sy, lambda: sy.dma_start(out=vht[:], in_=vh_t[r, :]), dma=True)
                sq.s(sy, lambda: sy.dma_start(out=lg[:], in_=lg_t[r, :]), dma=True)

            @block.vector
            def _(v):
                sq.s(v, lambda: v.tensor_sub(er[:], vt[:], vht[:]))
            for src, c0 in ((kt, 0), (vt, 16), (vht, 32), (er, 48)):
                @block.scalar
                def _(s, src=src):
                    sq.s(s, lambda: s.activation(sr[:], src[:], AF.Square))

                @block.vector
                def _(v, c0=c0):
                    for h in range(H):
                        sq.s(v, lambda h=h: v.reduce_sum(
                            co[:, c0 + h:c0 + h + 1],
                            sr[:, h * DK:(h + 1) * DK], AX.X))
            @block.vector
            def _(v):
                sq.s(v, lambda: v.reduce_max(co[:, 66:67], lg[:], AX.X))
                sq.s(v, lambda: v.tensor_scalar(
                    ex[:], lg[:], co[:, 66:67], None, ALU.subtract))

            @block.scalar
            def _(s):
                sq.s(s, lambda: s.activation(
                    ex[:], ex[:], AF.Exp, accum_out=co[:, 64:65]))

            @block.vector
            def _(v):
                sq.s(v, lambda: v.tensor_mul(ex[:], ex[:], lg[:]))
                sq.s(v, lambda: v.reduce_sum(co[:, 65:66], ex[:], AX.X))
                sq.s(v, lambda: v.reciprocal(co[:, 68:69], co[:, 64:65]))

            @block.scalar
            def _(s):
                sq.s(s, lambda: s.activation(co[:, 69:70], co[:, 64:65], AF.Ln))

            @block.vector
            def _(v):
                sq.s(v, lambda: v.tensor_mul(co[:, 70:71], co[:, 65:66], co[:, 68:69]))
                sq.s(v, lambda: v.tensor_add(co[:, 69:70], co[:, 69:70], co[:, 66:67]))
                sq.s(v, lambda: v.tensor_sub(co[:, 67:68], co[:, 69:70], co[:, 70:71]))
                sq.s(v, lambda: v.tensor_scalar_mul(
                    co[:, 67:68], co[:, 67:68], float(1.0 / np.log(64.0))))

            @block.scalar
            def _(s):
                sq.s(s, lambda: s.activation(co[:, 0:64], co[:, 0:64], AF.Ln))
                sq.s(s, lambda: s.activation(co[:, 0:64], co[:, 0:64], AF.Exp, scale=0.5))
            @block.vector
            def _(v):
                sq.s(v, lambda: v.tensor_scalar_add(co[:, 96:112], co[:, 0:16], 1e-6))
                sq.s(v, lambda: v.reciprocal(co[:, 96:112], co[:, 96:112]))
                for h in range(H):
                    sq.s(v, lambda h=h: v.tensor_scalar(
                        kt[:, h * DK:(h + 1) * DK], kt[:, h * DK:(h + 1) * DK],
                        co[:, 96 + h:97 + h], None, ALU.mult))

            @block.gpsimd
            def _(g, r=r):
                sq.s(g, lambda: g.dma_start(out=ku_o[r, :], in_=kt[:]), dma=True)

            @block.vector
            def _(v):
                sq.s(v, lambda: v.tensor_scalar_add(co[:, 112:128], co[:, 16:32], 1e-6))
                sq.s(v, lambda: v.reciprocal(co[:, 112:128], co[:, 112:128]))
                sq.s(v, lambda: v.tensor_mul(ft[:, 0:16], co[:, 48:64], co[:, 112:128]))
                sq.s(v, lambda: v.tensor_mul(co[:, 68:84], co[:, 16:32], co[:, 16:32]))
                sq.s(v, lambda: v.tensor_mul(co[:, 84:100], co[:, 32:48], co[:, 32:48]))
                sq.s(v, lambda: v.tensor_add(co[:, 68:84], co[:, 68:84], co[:, 84:100]))
                sq.s(v, lambda: v.tensor_mul(co[:, 84:100], co[:, 48:64], co[:, 48:64]))
                sq.s(v, lambda: v.tensor_sub(co[:, 68:84], co[:, 68:84], co[:, 84:100]))
                sq.s(v, lambda: v.tensor_scalar_mul(co[:, 68:84], co[:, 68:84], 0.5))
                sq.s(v, lambda: v.tensor_mul(co[:, 84:100], co[:, 16:32], co[:, 32:48]))
                sq.s(v, lambda: v.tensor_scalar_add(co[:, 84:100], co[:, 84:100], 1e-6))
                sq.s(v, lambda: v.reciprocal(co[:, 84:100], co[:, 84:100]))
                sq.s(v, lambda: v.tensor_mul(ft[:, 16:32], co[:, 68:84], co[:, 84:100]))

            @block.scalar
            def _(s):
                sq.s(s, lambda: s.activation(ft[:, 32:48], co[:, 48:64], AF.Ln, bias=1.0))

            @block.gpsimd
            def _(g, r=r):
                sq.s(g, lambda: g.dma_start(out=f4_o[r, :], in_=ft[:]), dma=True)
                sq.s(g, lambda: g.dma_start(out=ent_o[r, :], in_=co[:, 67:68]), dma=True)

        @block.sync
        def _(sy):
            sq.fin(sy)
    return nc


def build_L4():
    FW = H * BT // 8
    nc = bass.Bass("TRN2", target_bir_lowering=False)
    f8 = nc.dram_tensor("f8", [8, FW], F32, kind="ExternalInput").ap()
    w1 = nc.dram_tensor("w1", [8, 32], F32, kind="ExternalInput").ap()
    b1 = nc.dram_tensor("b1", [32, 1], F32, kind="ExternalInput").ap()
    w2 = nc.dram_tensor("w2", [32, 32], F32, kind="ExternalInput").ap()
    b2 = nc.dram_tensor("b2", [32, 1], F32, kind="ExternalInput").ap()
    wst = nc.dram_tensor("wst", [32, 5], F32, kind="ExternalInput").ap()
    b5 = nc.dram_tensor("b5", [5, 1], F32, kind="ExternalInput").ap()
    cmb = nc.dram_tensor("cmb", [5, 5], F32, kind="ExternalInput").ap()
    sg_o = nc.dram_tensor("sg_o", [5, FW], F32, kind="ExternalOutput").ap()
    with ExitStack() as st:
        sq = Seq(nc, st)
        fs = st.enter_context(nc.sbuf_tensor("fs", [8, 4096], F32))
        h1 = st.enter_context(nc.sbuf_tensor("h1", [32, 4096], F32))
        hf = st.enter_context(nc.sbuf_tensor("hf", [32, 4096], F32))
        r5 = st.enter_context(nc.sbuf_tensor("r5", [5, 4096], F32))
        sg = st.enter_context(nc.sbuf_tensor("sg", [5, 4096], F32))
        wsb = st.enter_context(nc.sbuf_tensor("wsb", [32, 80], F32))
        ps = st.enter_context(nc.psum_tensor("ps", [128, 512], F32))
        block = st.enter_context(nc.Block())

        @block.sync
        def _(sy):
            sq.s(sy, lambda: sy.dma_start(out=wsb[0:8, 0:32], in_=w1), dma=True)
            sq.s(sy, lambda: sy.dma_start(out=wsb[0:32, 32:64], in_=w2), dma=True)
            sq.s(sy, lambda: sy.dma_start(out=wsb[0:32, 64:69], in_=wst), dma=True)
            sq.s(sy, lambda: sy.dma_start(out=wsb[0:5, 69:74], in_=cmb), dma=True)
            sq.s(sy, lambda: sy.dma_start(out=wsb[0:32, 74:75], in_=b1), dma=True)
            sq.s(sy, lambda: sy.dma_start(out=wsb[0:32, 75:76], in_=b2), dma=True)
            sq.s(sy, lambda: sy.dma_start(out=wsb[0:5, 76:77], in_=b5), dma=True)

        for til in range(FW // 4096):
            c0 = til * 4096

            @block.sync
            def _(sy, c0=c0):
                sq.s(sy, lambda: sy.dma_start(
                    out=fs[:], in_=f8[:, c0:c0 + 4096]), dma=True)
            for sub in range(8):
                o0 = sub * 512

                @block.tensor
                def _(t, o0=o0):
                    sq.s(t, lambda: t.matmul(
                        ps[0:32, 0:512], wsb[0:8, 0:32], fs[:, o0:o0 + 512],
                        start=True, stop=True))

                @block.scalar
                def _(s, o0=o0):
                    sq.s(s, lambda: s.activation(
                        h1[:, o0:o0 + 512], ps[0:32, 0:512], AF.Silu,
                        bias=wsb[0:32, 74:75]))

                @block.tensor
                def _(t, o0=o0):
                    sq.s(t, lambda: t.matmul(
                        ps[0:32, 0:512], wsb[0:32, 32:64], h1[:, o0:o0 + 512],
                        start=True, stop=True))

                @block.scalar
                def _(s, o0=o0):
                    sq.s(s, lambda: s.activation(
                        hf[:, o0:o0 + 512], ps[0:32, 0:512], AF.Silu,
                        bias=wsb[0:32, 75:76]))

                @block.tensor
                def _(t, o0=o0):
                    sq.s(t, lambda: t.matmul(
                        ps[0:5, 0:512], wsb[0:32, 64:69], hf[:, o0:o0 + 512],
                        start=True, stop=True))

                @block.vector
                def _(v, o0=o0):
                    sq.s(v, lambda: v.tensor_scalar(
                        r5[:, o0:o0 + 512], ps[0:5, 0:512], wsb[0:5, 76:77],
                        None, ALU.add))

                @block.tensor
                def _(t, o0=o0):
                    sq.s(t, lambda: t.matmul(
                        ps[0:5, 0:512], wsb[0:5, 69:74], r5[:, o0:o0 + 512],
                        start=True, stop=True))

                @block.scalar
                def _(s, o0=o0):
                    sq.s(s, lambda: s.activation(
                        sg[:, o0:o0 + 512], ps[0:5, 0:512], AF.Tanh, scale=0.5))

                @block.vector
                def _(v, o0=o0):
                    sq.s(v, lambda: v.tensor_scalar(
                        sg[:, o0:o0 + 512], sg[:, o0:o0 + 512], 0.5, 0.5,
                        ALU.mult, ALU.add))

            @block.gpsimd
            def _(g, c0=c0):
                sq.s(g, lambda: g.dma_start(
                    out=sg_o[:, c0:c0 + 4096], in_=sg[:]), dma=True)

        @block.sync
        def _(sy):
            sq.fin(sy)
    return nc


def build_L5():
    nc = bass.Bass("TRN2", target_bir_lowering=False)
    kin = nc.dram_tensor("kin", [512, T], F32, kind="ExternalInput").ap()
    qin = nc.dram_tensor("qin", [512, T], F32, kind="ExternalInput").ap()
    vin = nc.dram_tensor("vin", [512, T], F32, kind="ExternalInput").ap()
    ampc = nc.dram_tensor("ampc", [128, 8 * NCH], F32, kind="ExternalInput").ap()
    betc = nc.dram_tensor("betc", [128, 8 * NCH], F32, kind="ExternalInput").ap()
    alogc = nc.dram_tensor("alogc", [128, 2], F32, kind="ExternalInput").ap()
    dtbc = nc.dram_tensor("dtbc", [128, 2], F32, kind="ExternalInput").ap()
    am = nc.dram_tensor("am", [10 * 128, 128], F32, kind="ExternalInput").ap()
    cm = nc.dram_tensor("cm", [10 * 128, 128], F32, kind="ExternalInput").ap()
    onest = nc.dram_tensor("onest", [128, 128], F32, kind="ExternalInput").ap()
    ident = nc.dram_tensor("ident", [128, 128], F32, kind="ExternalInput").ap()
    o_o = nc.dram_tensor("o_o", [8 * T, 128], F32, kind="ExternalOutput").ap()
    with ExitStack() as st:
        sq = Seq(nc, st)
        sb = lambda nm, shp: st.enter_context(nc.sbuf_tensor(nm, shp, F32))
        kc_ = sb("kc_", [128, 128]); qc_ = sb("qc_", [128, 128]); vc_ = sb("vc_", [128, 128])
        ampsb = sb("ampsb", [128, 8 * NCH]); betsb = sb("betsb", [128, 8 * NCH])
        gcol = sb("gcol", [128, 2]); gpos = sb("gpos", [128, 2])
        t1 = sb("t1", [128, 2]); t2 = sb("t2", [128, 2])
        amsb = sb("amsb", [128, 10 * 128]); cmsb = sb("cmsb", [128, 10 * 128])
        onesb = sb("onesb", [128, 128]); idsb = sb("idsb", [128, 128])
        ampb = sb("ampb", [128, 128])
        Dt = sb("Dt", [128, 11 * 128])
        Mt = sb("Mt", [128, 16 * 128])
        Nt = sb("Nt", [128, 128]); Aqt = sb("Aqt", [128, 128]); tmp = sb("tmp", [128, 128])
        pw = sb("pw", [128, 13 * 128])
        zz = sb("zz", [128, 128]); osb = sb("osb", [128, 128])
        khstb = sb("khstb", [128, 128]); s0d = sb("s0d", [128, 128])
        Ssb = sb("Ssb", [128, 8 * 128])
        psa = st.enter_context(nc.psum_tensor("psa", [128, 128], F32))
        psb = st.enter_context(nc.psum_tensor("psb", [128, 128], F32))
        psc = st.enter_context(nc.psum_tensor("psc", [128, 128], F32))
        block = st.enter_context(nc.Block())

        @block.sync
        def _(sy):
            sq.s(sy, lambda: sy.dma_start(out=ampsb[:], in_=ampc), dma=True)
            sq.s(sy, lambda: sy.dma_start(out=betsb[:], in_=betc), dma=True)
            sq.s(sy, lambda: sy.dma_start(out=t1[:], in_=alogc), dma=True)
            sq.s(sy, lambda: sy.dma_start(out=t2[:], in_=dtbc), dma=True)
            sq.s(sy, lambda: sy.dma_start(out=amsb[:], in_=am.rearrange("(a p) f -> p a f", p=128)), dma=True)
            sq.s(sy, lambda: sy.dma_start(out=cmsb[:], in_=cm.rearrange("(a p) f -> p a f", p=128)), dma=True)
            sq.s(sy, lambda: sy.dma_start(out=onesb[:], in_=onest), dma=True)
            sq.s(sy, lambda: sy.dma_start(out=idsb[:], in_=ident), dma=True)

        @block.scalar
        def _(s):
            sq.s(s, lambda: s.activation(gcol[:], t2[:], AF.Abs))
            sq.s(s, lambda: s.activation(gcol[:], gcol[:], AF.Exp, scale=-1.0))
            sq.s(s, lambda: s.activation(gcol[:], gcol[:], AF.Ln, bias=1.0))

        @block.vector
        def _(v):
            sq.s(v, lambda: v.tensor_relu(gpos[:], t2[:]))
            sq.s(v, lambda: v.tensor_add(gpos[:], gpos[:], gcol[:]))

        @block.scalar
        def _(s):
            sq.s(s, lambda: s.activation(t1[:], t1[:], AF.Exp))

        @block.vector
        def _(v):
            sq.s(v, lambda: v.tensor_mul(gpos[:], gpos[:], t1[:]))
            sq.s(v, lambda: v.tensor_scalar_mul(gcol[:], gpos[:], -1.0))
            sq.s(v, lambda: v.memset(Ssb[:], 0.0))

        for c in range(NCH):
            for s_ in range(4):
                hl = s_ // 2

                @block.sync
                def _(sy, c=c, s_=s_):
                    sq.s(sy, lambda: sy.dma_start(
                        out=kc_[:], in_=kin[s_ * 128:(s_ + 1) * 128, c * CH:(c + 1) * CH]), dma=True)
                    sq.s(sy, lambda: sy.dma_start(
                        out=qc_[:], in_=qin[s_ * 128:(s_ + 1) * 128, c * CH:(c + 1) * CH]), dma=True)
                    sq.s(sy, lambda: sy.dma_start(
                        out=vc_[:], in_=vin[s_ * 128:(s_ + 1) * 128, c * CH:(c + 1) * CH]), dma=True)
                for fs in range(2):
                    inst = s_ * 2 + fs
                    acol = ampsb[:, inst * NCH + c:inst * NCH + c + 1]
                    bcol = betsb[:, inst * NCH + c:inst * NCH + c + 1]
                    gc = gcol[:, hl:hl + 1]
                    gp = gpos[:, hl:hl + 1]
                    S = Ssb[:, inst * 128:(inst + 1) * 128]

                    @block.scalar
                    def _(s, acol=acol):
                        sq.s(s, lambda: s.activation(ampb[:], onesb[:], AF.Copy, scale=acol))
                    for ai in range(10):
                        @block.tensor
                        def _(t, ai=ai):
                            sq.s(t, lambda: t.matmul(
                                (psa if ai % 2 == 0 else psb)[:],
                                ampb[:], amsb[:, ai * 128:(ai + 1) * 128],
                                start=True, stop=True))

                        @block.scalar
                        def _(s, ai=ai, gc=gc):
                            sq.s(s, lambda: s.activation(
                                Dt[:, ai * 128:(ai + 1) * 128],
                                (psa if ai % 2 == 0 else psb)[:], AF.Exp, scale=gc))
                        if ai == 3:
                            @block.vector
                            def _(v, gp=gp):
                                sq.s(v, lambda: v.tensor_scalar(
                                    tmp[:], psb[:], gp, M_CL, ALU.mult, ALU.min))

                            @block.scalar
                            def _(s):
                                sq.s(s, lambda: s.activation(
                                    Dt[:, 10 * 128:11 * 128], tmp[:], AF.Exp))
                    @block.vector
                    def _(v):
                        for l in range(4):
                            sq.s(v, lambda l=l: v.scalar_tensor_tensor(
                                Mt[:, l * 128:(l + 1) * 128], kc_[:], -1.0,
                                Dt[:, l * 128:(l + 1) * 128], ALU.mult, ALU.mult))
                            sq.s(v, lambda l=l: v.tensor_mul(
                                Mt[:, (4 + l) * 128:(5 + l) * 128], qc_[:],
                                Dt[:, l * 128:(l + 1) * 128]))
                            sq.s(v, lambda l=l: v.tensor_mul(
                                Mt[:, (8 + l) * 128:(9 + l) * 128], kc_[:],
                                Dt[:, (4 + l) * 128:(5 + l) * 128]))
                        sq.s(v, lambda: v.tensor_mul(
                            Mt[:, 12 * 128:13 * 128], kc_[:], Dt[:, 10 * 128:11 * 128]))
                        sq.s(v, lambda: v.scalar_tensor_tensor(
                            Mt[:, 13 * 128:14 * 128], kc_[:], -1.0,
                            Dt[:, 8 * 128:9 * 128], ALU.mult, ALU.mult))
                        sq.s(v, lambda: v.tensor_mul(
                            Mt[:, 14 * 128:15 * 128], qc_[:], Dt[:, 8 * 128:9 * 128]))
                        sq.s(v, lambda: v.tensor_mul(
                            Mt[:, 15 * 128:16 * 128], kc_[:], Dt[:, 9 * 128:10 * 128]))
                    for which, dst in ((0, Nt), (1, Aqt)):
                        for l in range(5):
                            lhs = Mt[:, (8 + l) * 128:(9 + l) * 128] if l < 4 else Mt[:, 12 * 128:13 * 128]
                            rl = l if l < 4 else 3
                            rhs = (Mt[:, rl * 128:(rl + 1) * 128] if which == 0
                                   else Mt[:, (4 + rl) * 128:(5 + rl) * 128])
                            mi = which * 5 + l

                            @block.tensor
                            def _(t, lhs=lhs, rhs=rhs, l=l):
                                sq.s(t, lambda: t.matmul(
                                    (psa if l % 2 == 0 else psb)[:], lhs, rhs,
                                    start=True, stop=True))

                            @block.vector
                            def _(v, l=l, mi=mi, dst=dst, bcol=bcol):
                                if l == 0:
                                    sq.s(v, lambda: v.scalar_tensor_tensor(
                                        dst[:], (psa if l % 2 == 0 else psb)[:], bcol,
                                        cmsb[:, mi * 128:(mi + 1) * 128],
                                        ALU.mult, ALU.mult))
                                else:
                                    sq.s(v, lambda: v.scalar_tensor_tensor(
                                        tmp[:], (psa if l % 2 == 0 else psb)[:], bcol,
                                        cmsb[:, mi * 128:(mi + 1) * 128],
                                        ALU.mult, ALU.mult))
                                    sq.s(v, lambda: v.tensor_add(dst[:], dst[:], tmp[:]))
                    @block.tensor
                    def _(t):
                        sq.s(t, lambda: t.transpose(psc[:], Nt[:], idsb[:]))

                    @block.vector
                    def _(v):
                        sq.s(v, lambda: v.tensor_copy(pw[:, 7 * 128:8 * 128], psc[:]))
                        sq.s(v, lambda: v.tensor_copy(pw[:, 0:128], Nt[:]))
                    for kp in range(1, 7):
                        @block.tensor
                        def _(t, kp=kp):
                            sq.s(t, lambda: t.matmul(
                                psa[:], pw[:, (6 + kp) * 128:(7 + kp) * 128],
                                pw[:, (kp - 1) * 128:kp * 128], start=True, stop=True))

                        @block.vector
                        def _(v, kp=kp):
                            sq.s(v, lambda: v.tensor_copy(
                                pw[:, kp * 128:(kp + 1) * 128], psa[:]))
                        if kp < 6:
                            @block.tensor
                            def _(t, kp=kp):
                                sq.s(t, lambda: t.matmul(
                                    psb[:], pw[:, (kp - 1) * 128:kp * 128],
                                    pw[:, (6 + kp) * 128:(7 + kp) * 128],
                                    start=True, stop=True))

                            @block.vector
                            def _(v, kp=kp):
                                sq.s(v, lambda: v.tensor_copy(
                                    pw[:, (7 + kp) * 128:(8 + kp) * 128], psb[:]))
                    @block.tensor
                    def _(t):
                        sq.s(t, lambda: t.transpose(psc[:], vc_[:], idsb[:]))
                        sq.s(t, lambda: t.matmul(psc[:], Mt[:, 13 * 128:14 * 128],
                                                 S, start=False, stop=True))

                    @block.vector
                    def _(v):
                        sq.s(v, lambda: v.tensor_copy(zz[:], psc[:]))
                    for kp in (6, 5, 4, 3, 2, 1):
                        @block.tensor
                        def _(t, kp=kp):
                            sq.s(t, lambda: t.matmul(
                                psa[:], pw[:, kp * 128:(kp + 1) * 128], zz[:],
                                start=True, stop=True))

                        @block.vector
                        def _(v):
                            sq.s(v, lambda: v.tensor_add(zz[:], zz[:], psa[:]))

                    @block.tensor
                    def _(t):
                        sq.s(t, lambda: t.matmul(psa[:], pw[:, 0:128], zz[:],
                                                 start=True, stop=True))

                    @block.vector
                    def _(v):
                        sq.s(v, lambda: v.tensor_sub(zz[:], zz[:], psa[:]))
                    @block.tensor
                    def _(t):
                        sq.s(t, lambda: t.matmul(psb[:], Mt[:, 14 * 128:15 * 128], S,
                                                 start=True, stop=False))
                        sq.s(t, lambda: t.matmul(psb[:], Aqt[:], zz[:],
                                                 start=False, stop=True))

                    @block.vector
                    def _(v):
                        sq.s(v, lambda: v.tensor_copy(osb[:], psb[:]))

                    @block.gpsimd
                    def _(g, inst=inst, c=c):
                        sq.s(g, lambda: g.dma_start(
                            out=o_o[inst * T + c * CH:inst * T + (c + 1) * CH, :],
                            in_=osb[:]), dma=True)
                    @block.tensor
                    def _(t):
                        sq.s(t, lambda: t.transpose(psc[:], Mt[:, 15 * 128:16 * 128], idsb[:]))

                    @block.vector
                    def _(v, bcol=bcol):
                        sq.s(v, lambda: v.tensor_scalar(
                            khstb[:], psc[:], bcol, None, ALU.mult))
                        sq.s(v, lambda: v.tensor_scalar(
                            s0d[:], S, Dt[:, 8 * 128 + 127:8 * 128 + 128], None, ALU.mult))

                    @block.tensor
                    def _(t):
                        sq.s(t, lambda: t.matmul(psa[:], khstb[:], zz[:],
                                                 start=True, stop=False))
                        sq.s(t, lambda: t.matmul(psa[:], idsb[:], s0d[:],
                                                 start=False, stop=True))

                    @block.vector
                    def _(v, S=S):
                        sq.s(v, lambda: v.tensor_copy(S, psa[:]))

        @block.sync
        def _(sy):
            sq.fin(sy)
    return nc


def build_L6a():
    nc = bass.Bass("TRN2", target_bir_lowering=False)
    of_t = nc.dram_tensor("of_t", [PC, D], F32, kind="ExternalInput").ap()
    os_t = nc.dram_tensor("os_t", [PC, D], F32, kind="ExternalInput").ap()
    lam_t = nc.dram_tensor("lam_t", [PC, H], F32, kind="ExternalInput").ap()
    xgfm = nc.dram_tensor("xgfm", [128, PC], F32, kind="ExternalInput").ap()
    g2w = nc.dram_tensor("g2w", [128, D], F32, kind="ExternalInput").ap()
    g2bb = nc.dram_tensor("g2bb", [128, D], F32, kind="ExternalInput").ap()
    onbc = nc.dram_tensor("onbc", [128, D], F32, kind="ExternalInput").ap()
    wo = nc.dram_tensor("wo", [D, D], F32, kind="ExternalInput").ap()
    x_t = nc.dram_tensor("x_t", [PC, D], F32, kind="ExternalInput").ap()
    ident = nc.dram_tensor("ident", [128, 128], F32, kind="ExternalInput").ap()
    y1_o = nc.dram_tensor("y1_o", [PC, D], F32, kind="ExternalOutput").ap()
    with ExitStack() as st:
        sq = Seq(nc, st)
        sb = lambda nm, shp: st.enter_context(nc.sbuf_tensor(nm, shp, F32))
        oa = sb("oa", [128, D]); ob = sb("ob", [128, D]); lam = sb("lam", [128, H])
        xg = sb("xg", [128, PC]); gw = sb("gw", [128, D]); gb = sb("gb", [128, D])
        onb = sb("onb", [128, D]); sig = sb("sig", [128, D]); co = sb("co", [128, 64])
        ofm = sb("ofm", [128, 16 * 128]); wsb = sb("wsb", [128, 16 * 512])
        xt = sb("xt", [128, D]); idsb = sb("idsb", [128, 128])
        ps = st.enter_context(nc.psum_tensor("ps", [128, 512], F32))
        ps2 = st.enter_context(nc.psum_tensor("ps2", [128, 128], F32))
        block = st.enter_context(nc.Block())

        @block.sync
        def _(sy):
            sq.s(sy, lambda: sy.dma_start(out=xg[:], in_=xgfm), dma=True)
            sq.s(sy, lambda: sy.dma_start(out=idsb[:], in_=ident), dma=True)
            sq.s(sy, lambda: sy.dma_start(out=gw[:], in_=g2w), dma=True)
            sq.s(sy, lambda: sy.dma_start(out=gb[:], in_=g2bb), dma=True)
            sq.s(sy, lambda: sy.dma_start(out=onb[:], in_=onbc), dma=True)
        for tile in range(PC // 128):
            r = slice(tile * 128, (tile + 1) * 128)

            @block.sync
            def _(sy, r=r):
                sq.s(sy, lambda: sy.dma_start(out=oa[:], in_=of_t[r, :]), dma=True)
                sq.s(sy, lambda: sy.dma_start(out=ob[:], in_=os_t[r, :]), dma=True)
                sq.s(sy, lambda: sy.dma_start(out=lam[:], in_=lam_t[r, :]), dma=True)

            @block.vector
            def _(v):
                sq.s(v, lambda: v.tensor_sub(oa[:], oa[:], ob[:]))
                for h in range(H):
                    sq.s(v, lambda h=h: v.tensor_scalar(
                        oa[:, h * DV:(h + 1) * DV], oa[:, h * DV:(h + 1) * DV],
                        lam[:, h:h + 1], None, ALU.mult))
                sq.s(v, lambda: v.tensor_add(oa[:], oa[:], ob[:]))
            for og in range(4):
                @block.tensor
                def _(t, og=og, tile=tile):
                    sq.s(t, lambda: t.matmul(
                        ps[:, 0:512], xg[:, tile * 128:(tile + 1) * 128],
                        gw[:, og * 512:(og + 1) * 512], start=True, stop=True))

                @block.vector
                def _(v, og=og):
                    sq.s(v, lambda: v.tensor_add(
                        sig[:, og * 512:(og + 1) * 512], ps[:, 0:512],
                        gb[:, og * 512:(og + 1) * 512]))

            @block.scalar
            def _(s):
                sq.s(s, lambda: s.activation(sig[:], sig[:], AF.Tanh, scale=0.5))

            @block.vector
            def _(v):
                sq.s(v, lambda: v.tensor_scalar(sig[:], sig[:], 0.5, 0.5, ALU.mult, ALU.add))
            @block.scalar
            def _(s):
                sq.s(s, lambda: s.activation(ob[:], oa[:], AF.Square))

            @block.vector
            def _(v):
                for h in range(H):
                    sq.s(v, lambda h=h: v.reduce_sum(
                        co[:, h:h + 1], ob[:, h * DV:(h + 1) * DV], AX.X))
                sq.s(v, lambda: v.tensor_scalar(
                    co[:, 0:H], co[:, 0:H], 1.0 / DV, 1e-5, ALU.mult, ALU.add))

            @block.scalar
            def _(s):
                sq.s(s, lambda: s.activation(co[:, 0:H], co[:, 0:H], AF.Ln))
                sq.s(s, lambda: s.activation(co[:, 0:H], co[:, 0:H], AF.Exp, scale=-0.5))

            @block.vector
            def _(v):
                for h in range(H):
                    sq.s(v, lambda h=h: v.tensor_scalar(
                        oa[:, h * DV:(h + 1) * DV], oa[:, h * DV:(h + 1) * DV],
                        co[:, h:h + 1], None, ALU.mult))
                sq.s(v, lambda: v.tensor_mul(oa[:], oa[:], onb[:]))
                sq.s(v, lambda: v.tensor_mul(oa[:], oa[:], sig[:]))

            for kc in range(16):
                @block.tensor
                def _(t, kc=kc):
                    sq.s(t, lambda: t.transpose(
                        ps2[:], oa[:, kc * 128:(kc + 1) * 128], idsb[:]))

                @block.vector
                def _(v, kc=kc):
                    sq.s(v, lambda: v.tensor_copy(
                        ofm[:, kc * 128:(kc + 1) * 128], ps2[:]))

            @block.sync
            def _(sy, r=r):
                sq.s(sy, lambda: sy.dma_start(out=xt[:], in_=x_t[r, :]), dma=True)
            for og in range(4):
                @block.sync
                def _(sy, og=og):
                    sq.s(sy, lambda: sy.dma_start(
                        out=wsb[:].rearrange("p (a f) -> p a f", a=16),
                        in_=wo[:, og * 512:(og + 1) * 512].rearrange(
                            "(a p) f -> p a f", p=128)), dma=True)

                @block.tensor
                def _(t, og=og):
                    for kc in range(16):
                        sq.s(t, lambda kc=kc: t.matmul(
                            ps[:, 0:512], ofm[:, kc * 128:(kc + 1) * 128],
                            wsb[:, kc * 512:(kc + 1) * 512],
                            start=(kc == 0), stop=(kc == 15)))

                @block.vector
                def _(v, og=og):
                    sq.s(v, lambda: v.tensor_add(
                        ob[:, og * 512:(og + 1) * 512], ps[:, 0:512],
                        xt[:, og * 512:(og + 1) * 512]))
            @block.vector
            def _(v):
                sq.s(v, lambda: v.reduce_sum(co[:, 32:33], ob[:], AX.X))
                sq.s(v, lambda: v.tensor_scalar_mul(co[:, 32:33], co[:, 32:33], 1.0 / D))
                sq.s(v, lambda: v.tensor_scalar(
                    ob[:], ob[:], co[:, 32:33], None, ALU.subtract))

            @block.scalar
            def _(s):
                sq.s(s, lambda: s.activation(sig[:], ob[:], AF.Square))

            @block.vector
            def _(v):
                sq.s(v, lambda: v.reduce_sum(co[:, 33:34], sig[:], AX.X))
                sq.s(v, lambda: v.tensor_scalar(
                    co[:, 33:34], co[:, 33:34], 1.0 / D, 1e-5, ALU.mult, ALU.add))

            @block.scalar
            def _(s):
                sq.s(s, lambda: s.activation(co[:, 33:34], co[:, 33:34], AF.Ln))
                sq.s(s, lambda: s.activation(co[:, 33:34], co[:, 33:34], AF.Exp, scale=-0.5))

            @block.vector
            def _(v):
                sq.s(v, lambda: v.tensor_scalar(
                    ob[:], ob[:], co[:, 33:34], None, ALU.mult))

            @block.gpsimd
            def _(g, r=r):
                sq.s(g, lambda: g.dma_start(out=y1_o[r, :], in_=ob[:]), dma=True)

        @block.sync
        def _(sy):
            sq.fin(sy)
    return nc


def build_L6b():
    nc = bass.Bass("TRN2", target_bir_lowering=False)
    ofmT = nc.dram_tensor("ofmT", [D, PC], F32, kind="ExternalInput").ap()
    wo = nc.dram_tensor("wo", [D, D], F32, kind="ExternalInput").ap()
    x_t = nc.dram_tensor("x_t", [PC, D], F32, kind="ExternalInput").ap()
    y1_o = nc.dram_tensor("y1_o", [PC, D], F32, kind="ExternalOutput").ap()
    with ExitStack() as st:
        sq = Seq(nc, st)
        sb = lambda nm, shp: st.enter_context(nc.sbuf_tensor(nm, shp, F32))
        ofm = sb("ofm", [128, 16 * PC // 8])
        wsb = sb("wsb", [128, 16 * 512])
        acc = sb("acc", [128, D])
        xt = sb("xt", [128, D])
        co = sb("co", [128, 8])
        ps = st.enter_context(nc.psum_tensor("ps", [128, 512], F32))
        block = st.enter_context(nc.Block())
        for tile in range(PC // 128):
            r = slice(tile * 128, (tile + 1) * 128)

            @block.sync
            def _(sy, tile=tile, r=r):
                for kc in range(16):
                    sq.s(sy, lambda kc=kc: sy.dma_start(
                        out=ofm[:, kc * 128:(kc + 1) * 128],
                        in_=ofmT[kc * 128:(kc + 1) * 128, tile * 128:(tile + 1) * 128]),
                        dma=True)
                sq.s(sy, lambda: sy.dma_start(out=xt[:], in_=x_t[r, :]), dma=True)
            for og in range(4):
                @block.sync
                def _(sy, og=og):
                    sq.s(sy, lambda: sy.dma_start(
                        out=wsb[:].rearrange("p (a f) -> p a f", a=16),
                        in_=wo[:, og * 512:(og + 1) * 512].rearrange(
                            "(a p) f -> p a f", p=128)), dma=True)

                @block.tensor
                def _(t, og=og):
                    for kc in range(16):
                        sq.s(t, lambda kc=kc: t.matmul(
                            ps[:, 0:512], ofm[:, kc * 128:(kc + 1) * 128],
                            wsb[:, kc * 512:(kc + 1) * 512],
                            start=(kc == 0), stop=(kc == 15)))

                @block.vector
                def _(v, og=og):
                    sq.s(v, lambda: v.tensor_add(
                        acc[:, og * 512:(og + 1) * 512], ps[:, 0:512],
                        xt[:, og * 512:(og + 1) * 512]))
            @block.vector
            def _(v):
                sq.s(v, lambda: v.reduce_sum(co[:, 0:1], acc[:], AX.X))
                sq.s(v, lambda: v.tensor_scalar_mul(co[:, 0:1], co[:, 0:1], 1.0 / D))
                sq.s(v, lambda: v.tensor_scalar(
                    acc[:], acc[:], co[:, 0:1], None, ALU.subtract))

            @block.scalar
            def _(s):
                sq.s(s, lambda: s.activation(xt[:], acc[:], AF.Square))

            @block.vector
            def _(v):
                sq.s(v, lambda: v.reduce_sum(co[:, 1:2], xt[:], AX.X))
                sq.s(v, lambda: v.tensor_scalar(
                    co[:, 1:2], co[:, 1:2], 1.0 / D, 1e-5, ALU.mult, ALU.add))

            @block.scalar
            def _(s):
                sq.s(s, lambda: s.activation(co[:, 1:2], co[:, 1:2], AF.Ln))
                sq.s(s, lambda: s.activation(co[:, 1:2], co[:, 1:2], AF.Exp, scale=-0.5))

            @block.vector
            def _(v):
                sq.s(v, lambda: v.tensor_scalar(
                    acc[:], acc[:], co[:, 1:2], None, ALU.mult))

            @block.gpsimd
            def _(g, r=r):
                sq.s(g, lambda: g.dma_start(out=y1_o[r, :], in_=acc[:]), dma=True)

        @block.sync
        def _(sy):
            sq.fin(sy)
    return nc


def build_L7():
    nc = bass.Bass("TRN2", target_bir_lowering=False)
    yfmT = nc.dram_tensor("yfmT", [D, PC], F32, kind="ExternalInput").ap()
    y_t = nc.dram_tensor("y_t", [PC, D], F32, kind="ExternalInput").ap()
    wgT = nc.dram_tensor("wgT", [D, HID], I32, kind="ExternalInput").ap()
    wuT = nc.dram_tensor("wuT", [D, HID], I32, kind="ExternalInput").ap()
    wdT = nc.dram_tensor("wdT", [HID, D], I32, kind="ExternalInput").ap()
    sg_i = nc.dram_tensor("sg_i", [128, NHC], F32, kind="ExternalInput").ap()
    su_i = nc.dram_tensor("su_i", [128, NHC], F32, kind="ExternalInput").ap()
    sd_bc = nc.dram_tensor("sd_bc", [128, D], F32, kind="ExternalInput").ap()
    ones_c = nc.dram_tensor("ones_c", [128, 128], F32, kind="ExternalInput").ap()
    out_o = nc.dram_tensor("out_o", [PC, D], F32, kind="ExternalOutput").ap()
    gu_s = nc.dram_tensor("gu_s", [HID, PC], F32, kind="Internal").ap()
    with ExitStack() as st:
        sq = Seq(nc, st)
        sb = lambda nm, shp: st.enter_context(nc.sbuf_tensor(nm, shp, F32))
        yfm = sb("yfm", [128, 16 * PC])
        wsb = sb("wsb", [128, 16 * 128])
        wsb2 = sb("wsb2", [128, 16 * 128])
        rsb = sb("rsb", [128, PC])
        gatesb = sb("gatesb", [128, PC])
        upsb = sb("upsb", [128, PC])
        ssb = sb("ssb", [128, 2 * NHC])
        sdb = sb("sdb", [128, D])
        onesb = sb("onesb", [128, 128])
        yt = sb("yt", [128, D])
        accum = sb("accum", [128, D])
        rs2 = sb("rs2", [128, 8])
        co = sb("co", [128, 8])
        ps = st.enter_context(nc.psum_tensor("ps", [128, 512], F32))
        ps2 = st.enter_context(nc.psum_tensor("ps2", [128, 512], F32))
        psr = st.enter_context(nc.psum_tensor("psr", [128, 8], F32))
        block = st.enter_context(nc.Block())

        @block.sync
        def _(sy):
            for kc in range(16):
                sq.s(sy, lambda kc=kc: sy.dma_start(
                    out=yfm[:, kc * PC:(kc + 1) * PC],
                    in_=yfmT[kc * 128:(kc + 1) * 128, :]), dma=True)
            sq.s(sy, lambda: sy.dma_start(out=ssb[:, 0:NHC], in_=sg_i), dma=True)
            sq.s(sy, lambda: sy.dma_start(out=ssb[:, NHC:2 * NHC], in_=su_i), dma=True)
            sq.s(sy, lambda: sy.dma_start(out=sdb[:], in_=sd_bc), dma=True)
            sq.s(sy, lambda: sy.dma_start(out=onesb[:], in_=ones_c), dma=True)
        for grp in range(2):
            @block.tensor
            def _(t, grp=grp):
                for kc in range(16):
                    sq.s(t, lambda kc=kc, grp=grp: t.matmul(
                        (ps if grp == 0 else ps2)[:, 0:512], onesb[:],
                        yfm[:, kc * PC + grp * 512:kc * PC + grp * 512 + 512],
                        start=(kc == 0), stop=(kc == 15)))

            @block.vector
            def _(v, grp=grp):
                sq.s(v, lambda: v.tensor_scalar_mul(
                    rsb[:, grp * 512:(grp + 1) * 512],
                    (ps if grp == 0 else ps2)[:, 0:512], -128.0))
        for hc in range(NHC):
            @block.gpsimd
            def _(g, hc=hc):
                sq.s(g, lambda: g.dma_start(
                    out=wsb[:].rearrange("p (a f) -> p a f", a=16),
                    in_=wgT[:, hc * 128:(hc + 1) * 128].rearrange(
                        "(a p) f -> p a f", p=128)), dma=True)
                sq.s(g, lambda: g.dma_start(
                    out=wsb2[:].rearrange("p (a f) -> p a f", a=16),
                    in_=wuT[:, hc * 128:(hc + 1) * 128].rearrange(
                        "(a p) f -> p a f", p=128)), dma=True)
            for grp in range(2):
                g0 = grp * 512

                @block.tensor
                def _(t, hc=hc, g0=g0, grp=grp):
                    for kc in range(16):
                        sq.s(t, lambda kc=kc: t.matmul(
                            ps[:, 0:512], wsb[:, kc * 128:(kc + 1) * 128],
                            yfm[:, kc * PC + g0:kc * PC + g0 + 512],
                            start=(kc == 0), stop=(kc == 15)))
                    for kc in range(16):
                        sq.s(t, lambda kc=kc: t.matmul(
                            ps2[:, 0:512], wsb2[:, kc * 128:(kc + 1) * 128],
                            yfm[:, kc * PC + g0:kc * PC + g0 + 512],
                            start=(kc == 0), stop=(kc == 15)))

                @block.vector
                def _(v, g0=g0):
                    sq.s(v, lambda: v.tensor_add(
                        gatesb[:, g0:g0 + 512], ps[:, 0:512], rsb[:, g0:g0 + 512]))
                    sq.s(v, lambda: v.tensor_add(
                        upsb[:, g0:g0 + 512], ps2[:, 0:512], rsb[:, g0:g0 + 512]))

            @block.scalar
            def _(s, hc=hc):
                sq.s(s, lambda: s.activation(
                    gatesb[:], gatesb[:], AF.Silu, scale=ssb[:, hc:hc + 1]))

            @block.vector
            def _(v, hc=hc):
                sq.s(v, lambda: v.tensor_scalar(
                    upsb[:], upsb[:], ssb[:, NHC + hc:NHC + hc + 1], None, ALU.mult))
                sq.s(v, lambda: v.tensor_mul(gatesb[:], gatesb[:], upsb[:]))

            @block.tensor
            def _(t, hc=hc):
                for tl in range(8):
                    sq.s(t, lambda tl=tl, hc=hc: t.matmul(
                        psr[:, tl:tl + 1], gatesb[:, tl * 128:(tl + 1) * 128],
                        onesb[:, 0:1], start=(hc == 0 and tl == 0),
                        stop=(hc == NHC - 1)))

            @block.gpsimd
            def _(g, hc=hc):
                sq.s(g, lambda: g.dma_start(
                    out=gu_s[hc * 128:(hc + 1) * 128, :], in_=gatesb[:]), dma=True)

        @block.vector
        def _(v):
            sq.s(v, lambda: v.tensor_scalar_mul(rs2[:], psr[:], 128.0))
        for tl in range(8):
            r = slice(tl * 128, (tl + 1) * 128)
            for fg in range(4):
                @block.sync
                def _(sy, fg=fg):
                    pass
                for hc in range(NHC):
                    @block.gpsimd
                    def _(g, hc=hc, fg=fg):
                        if fg == 0 and True:
                            sq.s(g, lambda: g.dma_start(
                                out=wsb[:, 0:128], in_=gu_s[hc * 128:(hc + 1) * 128,
                                                           tl * 128:(tl + 1) * 128]),
                                dma=True)
                        sq.s(g, lambda: g.dma_start(
                            out=wsb2[:, 0:512],
                            in_=wdT[hc * 128:(hc + 1) * 128, fg * 512:(fg + 1) * 512]),
                            dma=True)
                        if fg != 0:
                            sq.s(g, lambda: g.dma_start(
                                out=wsb[:, 0:128], in_=gu_s[hc * 128:(hc + 1) * 128,
                                                            tl * 128:(tl + 1) * 128]),
                                dma=True)

                    @block.tensor
                    def _(t, hc=hc, fg=fg):
                        sq.s(t, lambda: t.matmul(
                            ps[:, 0:512], wsb[:, 0:128], wsb2[:, 0:512],
                            start=(hc == 0), stop=(hc == NHC - 1)))

                @block.vector
                def _(v, fg=fg, tl=tl):
                    sq.s(v, lambda: v.tensor_scalar(
                        accum[:, fg * 512:(fg + 1) * 512], ps[:, 0:512],
                        rs2[:, tl:tl + 1], None, ALU.subtract))
                    sq.s(v, lambda: v.tensor_mul(
                        accum[:, fg * 512:(fg + 1) * 512],
                        accum[:, fg * 512:(fg + 1) * 512],
                        sdb[:, fg * 512:(fg + 1) * 512]))

            @block.sync
            def _(sy, r=r):
                sq.s(sy, lambda: sy.dma_start(out=yt[:], in_=y_t[r, :]), dma=True)

            @block.vector
            def _(v):
                sq.s(v, lambda: v.tensor_add(accum[:], accum[:], yt[:]))
                sq.s(v, lambda: v.reduce_sum(co[:, 0:1], accum[:], AX.X))
                sq.s(v, lambda: v.tensor_scalar_mul(co[:, 0:1], co[:, 0:1], 1.0 / D))
                sq.s(v, lambda: v.tensor_scalar(
                    accum[:], accum[:], co[:, 0:1], None, ALU.subtract))

            @block.scalar
            def _(s):
                sq.s(s, lambda: s.activation(yt[:], accum[:], AF.Square))

            @block.vector
            def _(v):
                sq.s(v, lambda: v.reduce_sum(co[:, 1:2], yt[:], AX.X))
                sq.s(v, lambda: v.tensor_scalar(
                    co[:, 1:2], co[:, 1:2], 1.0 / D, 1e-5, ALU.mult, ALU.add))

            @block.scalar
            def _(s):
                sq.s(s, lambda: s.activation(co[:, 1:2], co[:, 1:2], AF.Ln))
                sq.s(s, lambda: s.activation(co[:, 1:2], co[:, 1:2], AF.Exp, scale=-0.5))

            @block.vector
            def _(v):
                sq.s(v, lambda: v.tensor_scalar(
                    accum[:], accum[:], co[:, 1:2], None, ALU.mult))

            @block.gpsimd
            def _(g, r=r):
                sq.s(g, lambda: g.dma_start(out=out_o[r, :], in_=accum[:]), dma=True)

        @block.sync
        def _(sy):
            sq.fin(sy)
    return nc


def _masks():
    am = np.zeros((10, 128, 128), np.float32)
    cm = np.zeros((10, 128, 128), np.float32)
    jj, tt = np.meshgrid(np.arange(128), np.arange(128), indexing="ij")
    for li, l in enumerate((64, 32, 16, 8)):
        same = (jj // l) == (tt // l)
        am[li] = (same & (jj <= tt)).astype(np.float32)
        am[4 + li] = (same & (jj > tt)).astype(np.float32)
        half = l
        same2 = (jj // (2 * l)) == (tt // (2 * l))
        band = same2 & ((jj // l) % 2 == 0) & ((tt // l) % 2 == 1)
        cm[li] = -band.astype(np.float32)
        cm[5 + li] = band.astype(np.float32)
    am[8] = (jj <= tt).astype(np.float32)
    am[9] = (jj > tt).astype(np.float32)
    same8 = (jj // 8) == (tt // 8)
    cm[4] = -(same8 & (jj < tt)).astype(np.float32)
    cm[9] = (same8 & (jj <= tt)).astype(np.float32)
    return (np.ascontiguousarray(am.reshape(10 * 128, 128)),
            np.ascontiguousarray(cm.reshape(10 * 128, 128)))


def _run(nc, maps):
    res = run_bass_kernel_spmd(nc, maps, core_ids=list(range(8)))
    return res.results


def kernel(**ins):
    f = lambda k: np.ascontiguousarray(np.asarray(ins[k], np.float32))
    x = f("L0_attn_hidden_states").reshape(BT, D)
    xb = x.reshape(B, T, D)
    xpad = np.concatenate([np.zeros((B, HLO, D), np.float32), xb], axis=1)
    cw = np.ascontiguousarray(np.concatenate(
        [f("L0_attn_conv_q_w"), f("L0_attn_conv_k_w"), f("L0_attn_conv_v_w")], 0))
    nc1 = build_L1()
    maps = []
    for c in range(8):
        b, t0 = c // 4, (c % 4) * PC
        win = np.ascontiguousarray(xpad[b, t0:t0 + PC + HLO, :].T)
        maps.append(dict(xT=win, wq=f("L0_attn_wq"), wk=f("L0_attn_wk"),
                         wv=f("L0_attn_wv"), cw=cw, unc=f("L0_attn_unc_w"),
                         g1=f("L0_attn_g1_w"), prox=f("L0_attn_proxy_w")))
    r1 = _run(nc1, maps)
    QS = np.concatenate([r["qs_o"] for r in r1], 1)
    K_ = np.concatenate([r["k_o"] for r in r1], 1)
    V_ = np.concatenate([r["v_o"] for r in r1], 1)
    VH = np.concatenate([r["vh_o"] for r in r1], 1)
    LG = np.concatenate([r["lg_o"] for r in r1], 1)
    XG = np.concatenate([r["xg_o"] for r in r1], 1)
    nc3 = build_L3()
    maps = []
    for c in range(8):
        sl = slice(c * PC, (c + 1) * PC)
        maps.append(dict(k_t=np.ascontiguousarray(K_[:, sl].T),
                         v_t=np.ascontiguousarray(V_[:, sl].T),
                         vh_t=np.ascontiguousarray(VH[:, sl].T),
                         lg_t=np.ascontiguousarray(LG[:, sl].T)))
    r3 = _run(nc3, maps)
    KU = np.concatenate([r["ku_o"] for r in r3], 0)
    F4 = np.concatenate([r["f4_o"] for r in r3], 0)
    ENT = np.concatenate([r["ent_o"] for r in r3], 0)[:, 0]
    he = f("L0_attn_head_embed")
    f8 = np.zeros((8, H * BT), np.float32)
    for h in range(H):
        sl = slice(h * BT, (h + 1) * BT)
        f8[0, sl] = ENT
        f8[1, sl] = F4[:, h]
        f8[2, sl] = F4[:, 16 + h]
        f8[3, sl] = F4[:, 32 + h]
        for j in range(4):
            f8[4 + j, sl] = he[h, j]
    wst = np.ascontiguousarray(np.stack(
        [f("L0_attn_bb_w"), f("L0_attn_bd_w"), f("L0_attn_lam_w"),
         f("L0_attn_ab_w"), f("L0_attn_ad_w")], 1))
    b5 = np.array([[f("L0_attn_bb_b")[0]], [f("L0_attn_bd_b")[0]],
                   [f("L0_attn_lam_b")[0]], [f("L0_attn_ab_b")[0]],
                   [f("L0_attn_ad_b")[0]]], np.float32)
    cmb = np.ascontiguousarray(np.array(
        [[1, 1, 0, 0, 0], [1, -1, 0, 0, 0], [0, 0, 1, 0, 0],
         [0, 0, 0, 1, 1], [0, 0, 0, 1, -1]], np.float32).T)
    nc4 = build_L4()
    FW = H * BT // 8
    maps = []
    for c in range(8):
        maps.append(dict(f8=np.ascontiguousarray(f8[:, c * FW:(c + 1) * FW]),
                         w1=f("L0_attn_mlp_w1"),
                         b1=f("L0_attn_mlp_b1").reshape(32, 1),
                         w2=f("L0_attn_mlp_w2"),
                         b2=f("L0_attn_mlp_b2").reshape(32, 1),
                         wst=wst, b5=b5, cmb=cmb))
    r4 = _run(nc4, maps)
    SG = np.concatenate([r["sg_o"] for r in r4], 1).reshape(5, H, BT)
    am, cm = _masks()
    ones_t = np.ones((128, 128), np.float32)
    ident = np.eye(128, dtype=np.float32)
    dtb = f("L0_attn_dt_bias").reshape(H, 128)
    alog = f("L0_attn_A_log")
    nc5 = build_L5()
    maps = []
    for c in range(8):
        kin = np.zeros((512, T), np.float32)
        qin = np.zeros((512, T), np.float32)
        vin = np.zeros((512, T), np.float32)
        ampc = np.zeros((128, 8 * NCH), np.float32)
        betc = np.zeros((128, 8 * NCH), np.float32)
        for hl in range(2):
            hg = 2 * c + hl
            for b in range(2):
                s_ = hl * 2 + b
                kin[s_ * 128:(s_ + 1) * 128] = KU.T[hg * DK:(hg + 1) * DK,
                                                    b * T:(b + 1) * T]
                qin[s_ * 128:(s_ + 1) * 128] = QS[hg * DK:(hg + 1) * DK,
                                                  b * T:(b + 1) * T]
                vin[s_ * 128:(s_ + 1) * 128] = V_[hg * DV:(hg + 1) * DV,
                                                  b * T:(b + 1) * T]
                for fs in range(2):
                    inst = s_ * 2 + fs
                    av = SG[3 + fs, hg, b * T:(b + 1) * T]
                    bv = SG[0 + fs, hg, b * T:(b + 1) * T]
                    ampc[:, inst * NCH:(inst + 1) * NCH] = av.reshape(NCH, 128).T
                    betc[:, inst * NCH:(inst + 1) * NCH] = bv.reshape(NCH, 128).T
        alogc = np.broadcast_to(alog[2 * c:2 * c + 2][None, :], (128, 2))
        maps.append(dict(kin=kin, qin=qin, vin=vin, ampc=ampc, betc=betc,
                         alogc=np.ascontiguousarray(alogc),
                         dtbc=np.ascontiguousarray(dtb[2 * c:2 * c + 2].T),
                         am=am, cm=cm, onest=ones_t, ident=ident))
    r5 = _run(nc5, maps)
    OF = np.zeros((BT, H, DV), np.float32)
    OS = np.zeros((BT, H, DV), np.float32)
    for c in range(8):
        oo = r5[c]["o_o"]
        for hl in range(2):
            for b in range(2):
                s_ = hl * 2 + b
                sl = slice(b * T, (b + 1) * T)
                OF[sl, 2 * c + hl] = oo[(s_ * 2) * T:(s_ * 2 + 1) * T]
                OS[sl, 2 * c + hl] = oo[(s_ * 2 + 1) * T:(s_ * 2 + 2) * T]
    LAM = np.ascontiguousarray(SG[2].T)
    onbc = np.broadcast_to(np.tile(f("L0_attn_onorm_w"), H)[None, :], (128, D))
    g2bb = np.broadcast_to(f("L0_attn_g2_b")[None, :], (128, D))
    nc6a = build_L6a()
    maps = []
    for c in range(8):
        sl = slice(c * PC, (c + 1) * PC)
        maps.append(dict(of_t=np.ascontiguousarray(OF.reshape(BT, D)[sl]),
                         os_t=np.ascontiguousarray(OS.reshape(BT, D)[sl]),
                         lam_t=np.ascontiguousarray(LAM[sl]),
                         xgfm=np.ascontiguousarray(XG[:, sl]),
                         g2w=f("L0_attn_g2_w"),
                         g2bb=np.ascontiguousarray(g2bb),
                         onbc=np.ascontiguousarray(onbc),
                         wo=f("L0_attn_wo"),
                         x_t=np.ascontiguousarray(x[sl]),
                         ident=np.eye(128, dtype=np.float32)))
    r6a = _run(nc6a, maps)
    Y1 = np.concatenate([r["y1_o"] for r in r6a], 0)
    wgT = np.ascontiguousarray(np.asarray(ins["L0_ffn_wq_gate"], np.int32).T)
    wuT = np.ascontiguousarray(np.asarray(ins["L0_ffn_wq_up"], np.int32).T)
    wdT = np.ascontiguousarray(np.asarray(ins["L0_ffn_wq_down"], np.int32).T)
    sg_i = np.ascontiguousarray(f("L0_ffn_s_gate").reshape(NHC, 128).T)
    su_i = np.ascontiguousarray(f("L0_ffn_s_up").reshape(NHC, 128).T)
    sd_bc = np.ascontiguousarray(np.broadcast_to(
        f("L0_ffn_s_down")[None, :], (128, D)))
    nc7 = build_L7()
    maps = []
    for c in range(8):
        sl = slice(c * PC, (c + 1) * PC)
        maps.append(dict(yfmT=np.ascontiguousarray(Y1[sl].T),
                         y_t=np.ascontiguousarray(Y1[sl]),
                         wgT=wgT, wuT=wuT, wdT=wdT,
                         sg_i=sg_i, su_i=su_i, sd_bc=sd_bc,
                         ones_c=np.ones((128, 128), np.float32)))
    r7 = _run(nc7, maps)
    OUT = np.concatenate([r["out_o"] for r in r7], 0)
    return OUT.reshape(B, T, D)
